# version 12
# baseline (speedup 1.0000x reference)
import os
os.environ.setdefault("NEURON_RT_DBG_RDH_CC", "0")

import numpy as np
import ml_dtypes
from contextlib import ExitStack

import concourse.bass as bass
import concourse.mybir as mybir
from concourse.bass_utils import run_bass_kernel_spmd

F32 = mybir.dt.float32
BF16 = mybir.dt.bfloat16
I32 = mybir.dt.int32
ALU = mybir.AluOpType
AF = mybir.ActivationFunctionType
AX = mybir.AxisListType

T = 8192
D = 2048
NH = 16
HS = 128
NCORE = 8
CH = 512
NCHK = 16
SP = 8
E = 8
IM = 1408
NIC = 11
ISH = 2816
NU = 10
NCC = 16
EPS = 1e-5
ISR = 0.08838834764831845
NEG = -1.0e9


def tb_of_chunk(c):
    return 2 * c if c < 8 else 2 * (15 - c) + 1



def _rect(ap):
    t = ap.tensor
    name = t.name
    try:
        off = int(ap.offset)
    except Exception:
        return (name, 0, 1 << 20, 0, 1 << 60)
    dims = list(ap.ap)
    if not dims:
        return (name, 0, 1 << 20, off, off + 1)
    space = str(ap.space)
    asz = mybir.dt.size(ap.dtype)
    if "DRAM" in space:
        lo = off
        hi = off
        for st, cnt in dims:
            if st >= 0:
                hi += st * (cnt - 1)
            else:
                lo += st * (cnt - 1)
        return (name, 0, 1 << 20, lo * asz, (hi + 1) * asz)
    row = 1
    for s in t.shape[1:]:
        row *= s
    tsz = mybir.dt.size(t.dtype)
    if tsz != asz:
        row = row * tsz // asz
    p0 = off // row
    f_lo = off % row
    f_hi = f_lo
    nump = 1
    for st, cnt in dims:
        if st != 0 and st % row == 0:
            nump = max(nump, (cnt - 1) * (st // row) + 1)
        else:
            if st >= 0:
                f_hi += st * (cnt - 1)
            else:
                f_lo += st * (cnt - 1)
    if "PSUM" in space:
        return (name, 0, 128, 0, 1 << 30)
    return (name, p0, p0 + nump, f_lo * asz, (f_hi + 1) * asz)


def _overlap(a, b):
    return (a[0] == b[0]) and not (
        a[2] <= b[1] or b[2] <= a[1] or a[4] <= b[3] or b[4] <= a[3]
    )


def _contains(a, b):
    return (a[0] == b[0]) and a[1] <= b[1] and a[2] >= b[2] and a[3] <= b[3] and a[4] >= b[4]


class Trk:
    def __init__(self, nc):
        self.nc = nc
        self.streams = {}
        self.cnt = {}
        self.waited = {}
        self.writes = {}
        self.reads = {}
        for name, eng, ekey, amt in [
            ("TE", nc.tensor, "te", 1),
            ("DVE", nc.vector, "dve", 1),
            ("ACT", nc.scalar, "act", 1),
            ("POOL", nc.gpsimd, "pool", 1),
            ("SPQ", nc.sync, "sp", 16),
            ("ACTQ", nc.scalar, "act", 16),
            ("POOLQ", nc.gpsimd, "pool", 16),
        ]:
            sem = nc.alloc_semaphore(f"trk_{name}")
            self.streams[name] = (eng, ekey, sem, amt)
            self.cnt[name] = 0

    def _deps_for(self, rect, want_writes, want_reads):
        deps = []
        if want_writes:
            for r, s, v in self.writes.get(rect[0], ()):
                if _overlap(rect, r):
                    deps.append((s, v))
        if want_reads:
            for r, s, v in self.reads.get(rect[0], ()):
                if _overlap(rect, r):
                    deps.append((s, v))
        return deps

    def _emit_waits(self, stream, deps):
        eng, ekey, ssem, _ = self.streams[stream]
        best = {}
        selfmax = 0
        for s, v in deps:
            if s == stream:
                if stream not in ("TE", "SPQ", "ACTQ", "POOLQ"):
                    selfmax = max(selfmax, v)
                continue
            _, dekey, dsem, _ = self.streams[s]
            if s == "TE" and ekey == "te":
                continue
            if dekey == ekey and dsem.name == ssem.name:
                continue
            key = (ekey, dsem.name)
            if self.waited.get(key, 0) >= v:
                continue
            cur = best.get(dsem.name)
            if cur is None or v > cur[1]:
                best[dsem.name] = (dsem, v, key)
        for dsem, v, key in best.values():
            eng.wait_ge(dsem, v)
            self.waited[key] = v
        if selfmax > 0:
            key = (ekey + "#self", ssem.name)
            if self.waited.get(key, 0) < selfmax:
                eng.wait_ge(ssem, selfmax)
                self.waited[key] = selfmax

    def _record(self, rect, stream, value, is_write):
        name = rect[0]
        if is_write:
            wl = self.writes.setdefault(name, [])
            wl[:] = [e for e in wl if not _contains(rect, e[0])]
            wl.append((rect, stream, value))
            rl = self.reads.setdefault(name, [])
            rl[:] = [e for e in rl if not _contains(rect, e[0])]
        else:
            rl = self.reads.setdefault(name, [])
            rl[:] = [e for e in rl
                     if not (_contains(rect, e[0]) and e[1] == stream)]
            rl.append((rect, stream, value))

    def op(self, stream, fn, reads=(), writes=(), inc=True):
        eng, ekey, sem, amt = self.streams[stream]
        rrects = [_rect(a) for a in reads]
        wrects = [_rect(a) for a in writes]
        deps = []
        for r in rrects:
            deps += self._deps_for(r, True, False)
        for r in wrects:
            deps += self._deps_for(r, True, True)
        self._emit_waits(stream, deps)
        inst = fn(eng)
        if inc:
            inst.then_inc(sem, amt)
            self.cnt[stream] += amt
        val = self.cnt[stream] + (0 if inc else amt)
        for r in rrects:
            self._record(r, stream, val, False)
        for r in wrects:
            self._record(r, stream, val, True)
        return inst

    def dma(self, queue, out, in_, **kw):
        return self.op(queue, lambda e: e.dma_start(out=out, in_=in_, **kw),
                       reads=[in_], writes=[out])

    def mm(self, out, lhsT, rhs, start=True, stop=True, inc=True, **kw):
        return self.op("TE",
                       lambda e: e.matmul(out, lhsT, rhs, start=start, stop=stop, **kw),
                       reads=[lhsT, rhs], writes=[out], inc=inc)

    def tr(self, out, in_, ident, inc=True):
        return self.op("TE", lambda e: e.transpose(out, in_, ident),
                       reads=[in_, ident], writes=[out], inc=inc)

    def act(self, out, in_, func, bias=0.0, scale=1.0, accum_out=None):
        reads = [in_]
        writes = [out]
        if hasattr(bias, "tensor"):
            reads.append(bias)
        if accum_out is not None:
            writes = [out, accum_out]
            return self.op("ACT",
                           lambda e: e.activation(out, in_, func, bias=bias,
                                                  scale=scale, accum_out=accum_out),
                           reads=reads, writes=writes)
        return self.op("ACT",
                       lambda e: e.activation(out, in_, func, bias=bias, scale=scale),
                       reads=reads, writes=writes)

    def copy(self, stream, out, in_):
        return self.op(stream, lambda e: e.tensor_copy(out, in_),
                       reads=[in_], writes=[out])

    def tt(self, stream, out, in0, in1, alu):
        return self.op(stream, lambda e: e.tensor_tensor(out, in0, in1, alu),
                       reads=[in0, in1], writes=[out])

    def ts(self, stream, out, in0, s1, s2, op0, op1=None):
        reads = [in0]
        if hasattr(s1, "tensor"):
            reads.append(s1)
        if s2 is not None and hasattr(s2, "tensor"):
            reads.append(s2)
        if op1 is None:
            return self.op(stream, lambda e: e.tensor_scalar(out, in0, s1, s2, op0),
                           reads=reads, writes=[out])
        return self.op(stream, lambda e: e.tensor_scalar(out, in0, s1, s2, op0, op1),
                       reads=reads, writes=[out])

    def stt(self, stream, out, in0, scalar, in1, op0, op1):
        reads = [in0, in1]
        if hasattr(scalar, "tensor"):
            reads.append(scalar)
        return self.op(stream,
                       lambda e: e.scalar_tensor_tensor(out, in0, scalar, in1, op0, op1),
                       reads=reads, writes=[out])

    def red(self, stream, out, in_, alu):
        return self.op(stream, lambda e: e.tensor_reduce(out, in_, AX.X, alu),
                       reads=[in_], writes=[out])

    def recip(self, out, in_):
        return self.op("DVE", lambda e: e.reciprocal(out, in_),
                       reads=[in_], writes=[out])

    def bn(self, out, in_, nchunks):
        n = in_.shape[-1] // nchunks
        for i in range(nchunks):
            sl = in_[:, i * n:(i + 1) * n]
            self.op("DVE", lambda e, i=i, sl=sl: e.bn_stats(
                self._bn_buf[:, i, :], sl),
                reads=[sl], writes=[self._bn_buf[:, i, :]])
        self.op("DVE", lambda e: e.bn_aggr(out, self._bn_buf[:, 0:nchunks, :]),
                reads=[self._bn_buf[:, 0:nchunks, :]], writes=[out])

    def collective(self, kind, aluop, ins, outs, replica_groups):
        sem = self.nc.alloc_semaphore(f"cc_{len(self.streams)}")
        eng, ekey, _, _ = self.streams["POOL"]
        deps = []
        for a in ins:
            deps += self._deps_for(_rect(a), True, False)
        for a in outs:
            deps += self._deps_for(_rect(a), True, True)
        self._emit_waits("POOL", deps)
        inst = eng.collective_compute(kind, aluop, replica_groups=replica_groups,
                                      ins=ins, outs=outs)
        inst.then_inc(sem)
        sname = f"CC{len(self.streams)}"
        self.streams[sname] = (eng, "pool", sem, 1)
        self.cnt[sname] = 1
        for a in outs:
            self._record(_rect(a), sname, 1, True)
        for a in ins:
            self._record(_rect(a), sname, 1, False)
        return inst

    def indirect(self, out, in_, idx):
        from concourse.bass import IndirectOffsetOnAxis
        return self.op("POOLQ",
                       lambda e: e.indirect_dma_start(
                           out, None, in_, IndirectOffsetOnAxis(idx, 0)),
                       reads=[in_, idx], writes=[out])

    def barrier(self):
        snap = [(sname, tup[2], self.cnt.get(sname, 0))
                for sname, tup in self.streams.items()
                if self.cnt.get(sname, 0) > 0 and not sname.startswith("CC")]
        for wname in ("TE", "DVE", "ACT", "POOL", "SPQ"):
            weng, wekey, _, _ = self.streams[wname]
            for sname, sem, v in snap:
                key = (wekey, sem.name)
                if self.waited.get(key, 0) < v:
                    weng.wait_ge(sem, v)
                    self.waited[key] = v

    def finish(self):
        eng, ekey, _, _ = self.streams["POOL"]
        for sname, (seng, sekey, sem, amt) in list(self.streams.items()):
            v = self.cnt.get(sname, 0)
            if v > 0:
                key = (ekey, sem.name)
                if self.waited.get(key, 0) < v:
                    eng.wait_ge(sem, v)
                    self.waited[key] = v



def build(debug_outputs=()):
    nc = bass.Bass("TRN2", target_bir_lowering=False)
    dbg = set(debug_outputs)

    x_own = nc.dram_tensor("x_own", [SP, 128, D], F32, kind="ExternalInput").ap()
    wq_m = nc.dram_tensor("wq_m", [128, NCC, 256], F32, kind="ExternalInput").ap()
    wk_m = nc.dram_tensor("wk_m", [128, NCC, 256], F32, kind="ExternalInput").ap()
    wv_m = nc.dram_tensor("wv_m", [128, NCC, 256], F32, kind="ExternalInput").ap()
    bqk_m = nc.dram_tensor("bqk_m", [128, 4], F32, kind="ExternalInput").ap()
    bv_m = nc.dram_tensor("bv_m", [1, 256], F32, kind="ExternalInput").ap()
    bo_r = nc.dram_tensor("bo_r", [1, D], F32, kind="ExternalInput").ap()
    wo_p = nc.dram_tensor("wo_p", [4, 128, NCC, 512], F32, kind="ExternalInput").ap()
    gwt_p = nc.dram_tensor("gwt_p", [128, NCC, 8], F32, kind="ExternalInput").ap()
    fw1 = nc.dram_tensor("fw1", [NU, NIC, 128, D], F32, kind="ExternalInput").ap()
    fw2 = nc.dram_tensor("fw2", [NU, NIC, 128, D], F32, kind="ExternalInput").ap()
    fw3 = nc.dram_tensor("fw3", [NU, 4, NIC, 128, 512], F32, kind="ExternalInput").ap()
    masks_d = nc.dram_tensor("masks", [128, 4, 512], BF16, kind="ExternalInput").ap()
    ident_d = nc.dram_tensor("ident", [128, 128], BF16, kind="ExternalInput").ap()
    ones_b_d = nc.dram_tensor("ones_b", [128, 128], BF16, kind="ExternalInput").ap()
    ones_f_d = nc.dram_tensor("ones_f", [128, 128], F32, kind="ExternalInput").ap()
    oidx_d = nc.dram_tensor("oidx", [128, 32], I32, kind="ExternalInput").ap()

    out_d = nc.dram_tensor("out", [SP, 128, D], F32, kind="ExternalOutput").ap()

    ln1_own = nc.dram_tensor("ln1_own", [2, D, 512], BF16, kind="Internal").ap()
    ln1_alla = nc.dram_tensor("ln1_alla", [NCORE * D, 512], BF16, kind="Internal", addr_space="Shared").ap()
    ln1_allb = nc.dram_tensor("ln1_allb", [NCORE * D, 512], BF16, kind="Internal", addr_space="Shared").ap()
    ccwarm_i = nc.dram_tensor("ccwarm_i", [16, 16], BF16, kind="Internal").ap()
    ccwarm_o = nc.dram_tensor("ccwarm_o", [128, 16], BF16, kind="Internal", addr_space="Shared").ap()
    x2_d = nc.dram_tensor("x2_d", [SP, 128, D], F32, kind="Internal").ap()

    o_own_a = nc.dram_tensor("o_own_a", [256, T // 2], BF16, kind="Internal").ap()
    o_own_b = nc.dram_tensor("o_own_b", [256, T // 4], BF16, kind="Internal").ap()
    o_own_c = nc.dram_tensor("o_own_c", [256, T // 4], BF16, kind="Internal").ap()
    o_all_a = nc.dram_tensor("o_all_a", [D, T // 2], BF16, kind="Internal", addr_space="Shared").ap()
    o_all_bc = nc.dram_tensor("o_all_bc", [2, D, T // 4], BF16, kind="Internal", addr_space="Shared").ap()

    dbg_aps = {}
    if "ln1_all" in dbg:
        dbg_aps["d_ln1"] = nc.dram_tensor("d_ln1", [NCORE * D, 1024], BF16,
                                          kind="ExternalOutput").ap()
    if "kqv" in dbg:
        dbg_aps["d_k"] = nc.dram_tensor("d_k", [128, 2, NCHK, 512], BF16,
                                        kind="ExternalOutput").ap()
        dbg_aps["d_q"] = nc.dram_tensor("d_q", [128, 2, NCHK, 512], BF16,
                                        kind="ExternalOutput").ap()
        dbg_aps["d_v"] = nc.dram_tensor("d_v", [128, 64, 256], BF16,
                                        kind="ExternalOutput").ap()
    if "o_all" in dbg:
        dbg_aps["d_o"] = nc.dram_tensor("d_o", [D, T], BF16,
                                        kind="ExternalOutput").ap()
    if "x2" in dbg:
        dbg_aps["d_x2"] = nc.dram_tensor("d_x2", [SP, 128, D], F32,
                                         kind="ExternalOutput").ap()
        dbg_aps["d_comb"] = nc.dram_tensor("d_comb", [128, SP, 8], F32,
                                           kind="ExternalOutput").ap()

    t = Trk(nc)
    RG = [list(range(NCORE))]

    with ExitStack() as top:
        et = top.enter_context
        ident = et(nc.sbuf_tensor("ident_sb", [128, 128], BF16))
        ones_b = et(nc.sbuf_tensor("ones_b_sb", [128, 128], BF16))
        ones_f = et(nc.sbuf_tensor("ones_f_sb", [128, 128], F32))
        bqk = et(nc.sbuf_tensor("bqk_sb", [128, 4], F32))
        bv_row = et(nc.sbuf_tensor("bv_row", [1, 256], F32))
        bv_bc = et(nc.sbuf_tensor("bv_bc", [128, 256], F32))
        gwt = et(nc.sbuf_tensor("gwt_sb", [128, NCC, 8], F32))
        gw_hi = et(nc.sbuf_tensor("gw_hi", [128, NCC, 8], BF16))
        gw_lo = et(nc.sbuf_tensor("gw_lo", [128, NCC, 8], BF16))
        shavg = et(nc.sbuf_tensor("shavg", [128, 1], F32))
        sh_tmp = et(nc.sbuf_tensor("sh_tmp", [128, 2], F32))
        sh_row = et(nc.sbuf_tensor("sh_row", [1, 2], F32))
        oidx = et(nc.sbuf_tensor("oidx_sb", [128, 32], I32))
        comb = et(nc.sbuf_tensor("comb_sb", [128, SP, 8], F32))
        bnb = et(nc.sbuf_tensor("bn_buf", [128, 4, 6], F32))
        mv = et(nc.sbuf_tensor("mv_sb", [128, 2, 2], F32))
        rstd = et(nc.sbuf_tensor("rstd_sb", [128, 2], F32))
        eps_t = et(nc.sbuf_tensor("eps_t", [128, 1], F32))
        t._bn_buf = bnb
        t.op("DVE", lambda e: e.memset(eps_t[:, :], EPS), writes=[eps_t[:, :]])

        o_scope = ExitStack()
        oT = o_scope.enter_context(nc.sbuf_tensor("oT_sb", [128, NCC, 1024], BF16, side="right"))
        w_scope = ExitStack()
        ew = w_scope.enter_context
        wst = ew(nc.sbuf_tensor("w_stage", [128, 8, 256], F32))
        wqb = ew(nc.sbuf_tensor("wq_b", [128, NCC, 256], BF16))
        wkb = ew(nc.sbuf_tensor("wk_b", [128, NCC, 256], BF16))
        wvb = ew(nc.sbuf_tensor("wv_b", [128, NCC, 256], BF16))

        t.dma("ACTQ", ident[:, :], ident_d)
        t.dma("ACTQ", ones_b[:, :], ones_b_d)
        t.dma("ACTQ", ones_f[:, :], ones_f_d)
        t.dma("ACTQ", bqk[:, :], bqk_m)
        t.dma("ACTQ", bv_row[:, :], bv_m)
        t.dma("ACTQ", gwt[:, :, :], gwt_p)
        t.dma("ACTQ", oidx[:, :], oidx_d)
        t.collective("AllGather", ALU.bypass, [ccwarm_i], [ccwarm_o], RG)


        def ln_stats(src, slot):
            t.bn(mv[:, slot, :], src, 4)
            t.act(rstd[:, slot:slot + 1], mv[:, slot, 1:2], AF.Ln, bias=eps_t[:, :])
            t.act(rstd[:, slot:slot + 1], rstd[:, slot:slot + 1], AF.Exp, scale=-0.5)

        with ExitStack() as ph:
            e = ph.enter_context
            x_ring = e(nc.sbuf_tensor("x_ring", [128, 2, D], F32))
            ln1_tok = e(nc.sbuf_tensor("ln1_tok", [128, SP, D], BF16))
            tr_st = e(nc.sbuf_tensor("tr_st", [128, 2, 1024], BF16))
            cps = e(nc.psum_tensor("cps", [128, 512], F32))
            trp = [e(nc.psum_tensor(f"trp{i}", [128, 1024], BF16)) for i in range(2)]

            t.mm(cps[:, 0:256], ones_f[0:1, :], bv_row[0:1, :])
            t.copy("DVE", bv_bc[:, :], cps[:, 0:256])
            t.stt("DVE", sh_tmp[:, :], bqk[:, 0:2], -ISR, bqk[:, 2:4],
                  ALU.mult, ALU.mult)
            t.mm(cps[0:1, 0:2], ones_f[:, 0:1], sh_tmp[:, :])
            t.copy("DVE", sh_row[0:1, :], cps[0:1, 0:2])
            t.red("DVE", sh_row[0:1, 0:1], sh_row[0:1, :], ALU.add)
            t.ts("DVE", sh_row[0:1, 0:1], sh_row[0:1, 0:1], 0.5, None, ALU.mult)
            t.mm(cps[:, 0:1], ones_f[0:1, :], sh_row[0:1, 0:1])
            t.copy("DVE", shavg[:, :], cps[:, 0:1])
            t.copy("DVE", gw_hi[:, :, :], gwt[:, :, :])
            t.tt("DVE", gw_lo[:, :, :], gwt[:, :, :], gw_hi[:, :, :], ALU.subtract)

            for s in range(SP):
                xs = x_ring[:, s % 2, :]
                t.dma("SPQ", xs, x_own[s])
                ln_stats(xs, s % 2)
                t.ts("DVE", ln1_tok[:, s, :], xs,
                     mv[:, s % 2, 0:1], rstd[:, s % 2:s % 2 + 1],
                     ALU.subtract, ALU.mult)
            for h in range(2):
                for cc in range(NCC):
                    pb = trp[cc % 2]
                    for si in range(4):
                        sa = h * 4 + si
                        t.tr(pb[:, si * 128:(si + 1) * 128],
                             ln1_tok[:, sa, cc * 128:(cc + 1) * 128], ident[:, :],
                             inc=(si == 3))
                    st = tr_st[:, cc % 2, 0:512]
                    t.copy("DVE", st, pb[:, 0:512])
                    t.dma("ACTQ", ln1_own[h, cc * 128:(cc + 1) * 128, :], st)
                t.collective("AllGather", ALU.bypass, [ln1_own[h]],
                             [ln1_alla if h == 0 else ln1_allb], RG)
            for wsrc, wdst in ((wq_m, wqb), (wk_m, wkb), (wv_m, wvb)):
                for hh in range(2):
                    t.dma("SPQ", wst[:, :, :], wsrc[:, hh * 8:(hh + 1) * 8, :])
                    t.copy("DVE", wdst[:, hh * 8:(hh + 1) * 8, :], wst[:, :, :])

        t.barrier()

        kqv_scope = ExitStack()
        e = kqv_scope.enter_context
        masks = e(nc.sbuf_tensor("masks_sb", [128, 4, 512], BF16))
        k_res = e(nc.sbuf_tensor("k_res", [128, 2, NCHK, 512], BF16))
        q_res = e(nc.sbuf_tensor("q_res", [128, 2, NCHK, 512], BF16))
        v_res = e(nc.sbuf_tensor("v_res", [128, 64, 256], BF16))

        t.dma("ACTQ", masks[:, :, :], masks_d)
        with ExitStack() as ph:
            e = ph.enter_context
            lnr = e(nc.sbuf_tensor("lnall_ring", [128, 2, NCC, 512], BF16))
            kp = [e(nc.psum_tensor(f"kp{i}", [128, 512], F32)) for i in range(2)]
            qp = [e(nc.psum_tensor(f"qp{i}", [128, 512], F32)) for i in range(2)]
            vp = [e(nc.psum_tensor(f"vp{i}", [128, 512], F32)) for i in range(2)]

            def lnall_block(tb):
                src = ln1_alla if tb % 2 == 0 else ln1_allb
                r = tb // 2
                return src.rearrange("(r cc p) q -> p r cc q", p=128, cc=NCC)[:, r]

            tb_order = list(range(0, NCHK, 2)) + list(range(1, NCHK, 2))
            t.dma("SPQ", lnr[:, 0, :, :], lnall_block(tb_order[0]))
            for ti in range(NCHK):
                tb = tb_order[ti]
                if ti + 1 < NCHK:
                    t.dma("SPQ", lnr[:, (ti + 1) % 2, :, :],
                          lnall_block(tb_order[ti + 1]))
                ln = lnr[:, ti % 2, :, :]
                for hp in range(2):
                    for cc in range(NCC):
                        t.mm(kp[hp][:, :], wkb[:, cc, hp * 128:(hp + 1) * 128],
                             ln[:, cc, :], start=(cc == 0), stop=(cc == NCC - 1),
                             inc=(cc == NCC - 1))
                    t.act(k_res[:, hp, tb, :], kp[hp][:, :], AF.Identity,
                          bias=bqk[:, 2 + hp:3 + hp])
                    for cc in range(NCC):
                        t.mm(qp[hp][:, :], wqb[:, cc, hp * 128:(hp + 1) * 128],
                             ln[:, cc, :], start=(cc == 0), stop=(cc == NCC - 1),
                             inc=(cc == NCC - 1))
                    t.act(q_res[:, hp, tb, :], qp[hp][:, :], AF.Identity,
                          bias=bqk[:, hp:hp + 1])
                for j in range(4):
                    p = vp[j % 2]
                    for cc in range(NCC):
                        t.mm(p[:, 0:256], ln[:, cc, j * 128:(j + 1) * 128],
                             wvb[:, cc, :], start=(cc == 0), stop=(cc == NCC - 1),
                             inc=(cc == NCC - 1))
                    t.tt("DVE", v_res[:, tb * 4 + j, :], p[:, 0:256], bv_bc[:, :],
                         ALU.add)
            if "kqv" in dbg:
                t.dma("ACTQ", dbg_aps["d_k"], k_res[:, :, :, :])
                t.dma("ACTQ", dbg_aps["d_q"], q_res[:, :, :, :])
                t.dma("ACTQ", dbg_aps["d_v"], v_res[:, :, :])

        t.barrier()

        with ExitStack() as ph:
            e = ph.enter_context
            p_ring = e(nc.sbuf_tensor("p_ring", [128, 4, 1024], BF16))
            o_st = e(nc.sbuf_tensor("o_st", [128, 2, 2, 512], BF16))
            rcp = e(nc.sbuf_tensor("rcp_sb", [1, 2, 512], F32))
            r_sb = e(nc.sbuf_tensor("r_sb", [128, 2, 512], F32))
            lg = [e(nc.psum_tensor(f"lg{i}", [128, 1024], F32)) for i in range(2)]
            ops = [e(nc.psum_tensor(f"op{i}", [128, 512], F32)) for i in range(2)]
            sps = [e(nc.psum_tensor(f"sp{i}", [128, 512], F32)) for i in range(2)]

            o_flat_a = o_all_a.rearrange("r (k c) -> (r k) c", c=512)
            o_flat_bc = o_all_bc.rearrange("h r (k c) -> (h r k) c", c=512)

            def norm_store(qb):
                for hp in range(2):
                    t.recip(rcp[0:1, hp, :], sps[hp][0:1, :])
                    t.mm(sps[hp][:, :], ones_f[0:1, :], rcp[0:1, hp, :])
                    t.copy("DVE", r_sb[:, hp, :], sps[hp][:, :])
                    t.tt("DVE", o_st[:, qb % 2, hp, :], ops[hp][:, :],
                         r_sb[:, hp, :], ALU.mult)
                if qb < 8:
                    half, cq = o_own_a, qb
                elif qb < 12:
                    half, cq = o_own_b, qb - 8
                else:
                    half, cq = o_own_c, qb - 12
                t.dma("ACTQ",
                      half.rearrange("(h p) q -> p h q", p=128)[
                          :, :, cq * 512:(cq + 1) * 512],
                      o_st[:, qb % 2, :, :])

            def sums_av(n, qb_n, tc, j, first, last):
                for hp in range(2):
                    phalf = p_ring[:, n % 4, hp * 512:(hp + 1) * 512]
                    t.mm(sps[hp][0:64, :], ones_b[:, 0:64], phalf,
                         start=first, stop=last, inc=False)
                    t.mm(ops[hp][:, :],
                         v_res[:, tc * 4 + j, hp * 128:(hp + 1) * 128],
                         phalf, start=first, stop=last, inc=(hp == 1))

            tiles = []
            for qb in range(NCHK):
                for c in range(qb + 1):
                    for j in range(4):
                        tiles.append((qb, c, j, c == 0 and j == 0,
                                      c == qb and j == 3))
            pend = None
            pend_norm = None
            gn = 0
            for (qb, c, j, first, last) in tiles:
                g = lg[gn % 2]
                ps = p_ring[:, gn % 4, :]
                tc = tb_of_chunk(c)
                tq = tb_of_chunk(qb)
                t.mm(g[:, 0:512], k_res[:, 0, tc, j * 128:(j + 1) * 128],
                     q_res[:, 0, tq, :], inc=False)
                t.mm(g[:, 512:1024], k_res[:, 1, tc, j * 128:(j + 1) * 128],
                     q_res[:, 1, tq, :], inc=True)
                if c == qb:
                    gv = g[:, :].rearrange("p (h q) -> p h q", h=2)
                    t.tt("DVE", gv, gv,
                         masks[:, j:j + 1, :].broadcast_to([128, 2, 512]),
                         ALU.add)
                t.act(ps, g[:, :], AF.Exp, bias=shavg[:, :], scale=ISR)
                if pend_norm is not None:
                    norm_store(pend_norm)
                    if pend_norm == 7:
                        t.collective("AllGather", ALU.bypass,
                                     [o_own_a], [o_all_a], RG)
                        for cc in range(NCC):
                            t.indirect(oT[:, cc, 0:512], o_flat_a,
                                       oidx[:, cc * 2:cc * 2 + 1])
                    elif pend_norm == 11:
                        t.collective("AllGather", ALU.bypass,
                                     [o_own_b], [o_all_bc[0]], RG)
                    pend_norm = None
                if pend is not None:
                    sums_av(pend[5], pend[0], tb_of_chunk(pend[1]), pend[2],
                            pend[3], pend[4])
                    if pend[4]:
                        pend_norm = pend[0]
                pend = (qb, c, j, first, last, gn)
                gn += 1
            sums_av(pend[5], pend[0], tb_of_chunk(pend[1]), pend[2],
                    pend[3], pend[4])
            norm_store(pend[0])
            t.collective("AllGather", ALU.bypass, [o_own_c], [o_all_bc[1]], RG)
            for cc in range(NCC):
                t.indirect(oT[:, cc, 512:1024], o_flat_bc,
                           oidx[:, cc * 2 + 1:cc * 2 + 2])
            if "o_all" in dbg:
                t.dma("ACTQ", dbg_aps["d_o"][:, 0:T // 2], o_all_a)
                t.dma("ACTQ", dbg_aps["d_o"][:, T // 2:3 * T // 4], o_all_bc[0])
                t.dma("ACTQ", dbg_aps["d_o"][:, 3 * T // 4:T], o_all_bc[1])

        kqv_scope.close()
        w_scope.close()
        t.barrier()

        ln2t = top.enter_context(nc.sbuf_tensor("ln2t", [128, NCC, 1024], BF16))
        with ExitStack() as ph:
            e = ph.enter_context
            attn_q = e(nc.sbuf_tensor("attn_q", [128, 2, D], F32))
            wo_st = e(nc.sbuf_tensor("wo_st", [128, 2, 4, 512], F32))
            wo_all = e(nc.sbuf_tensor("wo_all", [128, 4, NCC, 512], BF16))
            bo_bc = e(nc.sbuf_tensor("bo_bc", [128, D], F32))
            x_r2 = e(nc.sbuf_tensor("x_r2", [128, 2, D], F32))
            hi_r = e(nc.sbuf_tensor("hi_r", [128, 2, D], BF16))
            lo_s = e(nc.sbuf_tensor("lo_s", [128, 2, NCC, 128], BF16))
            g_t = e(nc.sbuf_tensor("g_t", [128, 8, 8], F32))
            nmr_t = e(nc.sbuf_tensor("nmr_t", [128, 2], F32))
            opp = [e(nc.psum_tensor(f"opp{i}", [128, 512], F32)) for i in range(2)]
            tp = e(nc.psum_tensor("tp", [128, 1024], BF16))
            gp = e(nc.psum_tensor("gp", [128, 512], F32))

            for db in range(4):
                for q4 in range(4):
                    st = wo_st[:, q4 % 2, :, :]
                    t.dma("SPQ", st, wo_p[db, :, q4 * 4:(q4 + 1) * 4, :])
                    t.copy("DVE", wo_all[:, db, q4 * 4:(q4 + 1) * 4, :], st)
            bo_stage = attn_q[0:1, 1, :]
            t.dma("ACTQ", bo_stage, bo_r)
            for b in range(4):
                t.mm(gp[:, :], ones_f[0:1, :], attn_q[0:1, 1, b * 512:(b + 1) * 512])
                t.copy("DVE", bo_bc[:, b * 512:(b + 1) * 512], gp[:, :])

            def chain_a(sa):
                sl = sa % 2
                a = attn_q[:, sl, :]
                xs = x_r2[:, sl, :]
                t.dma("SPQ", xs, x_own[sa])
                ln_stats(a, sl)
                t.ts("DVE", nmr_t[:, sl:sl + 1], mv[:, sl, 0:1],
                     rstd[:, sl:sl + 1], -1.0, ALU.mult, ALU.mult)
                t.act(a, a, AF.Identity, bias=nmr_t[:, sl:sl + 1],
                      scale=rstd[:, sl:sl + 1])
                t.tt("POOL", a, a, xs, ALU.add)
                t.dma("ACTQ", x2_d[sa], a)
                if "x2" in dbg:
                    t.dma("ACTQ", dbg_aps["d_x2"][sa], a)
                ln_stats(a, sl)
                hi = hi_r[:, sl, :]
                tmp = x_r2[:, sl, :]
                t.ts("DVE", nmr_t[:, sl:sl + 1], mv[:, sl, 0:1],
                     rstd[:, sl:sl + 1], -1.0, ALU.mult, ALU.mult)
                t.act(tmp, a, AF.Identity, bias=nmr_t[:, sl:sl + 1],
                      scale=rstd[:, sl:sl + 1])
                t.act(hi, tmp, AF.Copy)
                t.tt("DVE", lo_s[:, sl, :, :],
                     tmp[:, :].rearrange("p (c x) -> p c x", x=128),
                     hi[:, :].rearrange("p (c x) -> p c x", x=128),
                     ALU.subtract)

            def chain_b(sa):
                sl = sa % 2
                hi = hi_r[:, sl, :]
                for ccg in range(4):
                    for c4 in range(4):
                        cc = ccg * 4 + c4
                        t.tr(tp[:, c4 * 256:c4 * 256 + 128],
                             hi[:, cc * 128:(cc + 1) * 128], ident[:, :],
                             inc=False)
                        t.tr(tp[:, c4 * 256 + 128:c4 * 256 + 256],
                             lo_s[:, sl, cc, :], ident[:, :], inc=(c4 == 3))
                    tpv = tp[:, :].rearrange("p (c two q) -> p c two q",
                                             c=4, two=2)
                    t.copy("DVE",
                           ln2t[:, ccg * 4:(ccg + 1) * 4,
                                sa * 128:(sa + 1) * 128],
                           tpv[:, :, 0, :])
                    t.copy("DVE", lo_s[:, sl, ccg * 4:(ccg + 1) * 4, :],
                           tpv[:, :, 1, :])
                for cc in range(NCC):
                    hiT = ln2t[:, cc, sa * 128:(sa + 1) * 128]
                    t.mm(gp[:, 0:8], hiT, gw_hi[:, cc, :],
                         start=(cc == 0), stop=False, inc=False)
                    t.mm(gp[:, 0:8], hiT, gw_lo[:, cc, :],
                         start=False, stop=False, inc=False)
                    t.mm(gp[:, 0:8], lo_s[:, sl, cc, :], gw_hi[:, cc, :],
                         start=False, stop=(cc == NCC - 1),
                         inc=(cc == NCC - 1))
                gl = g_t[:, 0, :]
                t.copy("DVE", gl, gp[:, 0:8])
                m1 = g_t[:, 1, 0:1]
                t.red("DVE", m1, gl, ALU.max)
                t1 = g_t[:, 2, :]
                t.ts("DVE", t1, gl, m1, None, ALU.subtract)
                et_ = g_t[:, 3, :]
                t.act(et_, t1, AF.Exp)
                se = g_t[:, 1, 1:2]
                t.red("DVE", se, et_, ALU.add)
                rse = g_t[:, 1, 2:3]
                t.recip(rse, se)
                pt = g_t[:, 4, :]
                t.ts("DVE", pt, et_, rse, None, ALU.mult)
                m1p = g_t[:, 1, 3:4]
                t.red("DVE", m1p, pt, ALU.max)
                nmax = g_t[:, 5, :]
                t.ts("DVE", nmax, pt, m1p, None, ALU.is_lt)
                pm = g_t[:, 6, :]
                t.tt("DVE", pm, pt, nmax, ALU.mult)
                m2p = g_t[:, 1, 4:5]
                t.red("DVE", m2p, pm, ALU.max)
                keep = g_t[:, 7, :]
                t.ts("DVE", keep, pt, m2p, None, ALU.is_ge)
                t.tt("DVE", comb[:, sa, :], pt, keep, ALU.mult)

            for sa in range(SP):
                for db in range(4):
                    p = opp[db % 2]
                    for hc in range(NCC):
                        t.mm(p[:, :], oT[:, hc, sa * 128:(sa + 1) * 128],
                             wo_all[:, db, hc, :], start=(hc == 0),
                             stop=(hc == NCC - 1), inc=(hc == NCC - 1))
                    t.tt("DVE", attn_q[:, sa % 2, db * 512:(db + 1) * 512],
                         p[:, :], bo_bc[:, db * 512:(db + 1) * 512], ALU.add)
                chain_a(sa)
                if sa > 0:
                    chain_b(sa - 1)
            chain_b(SP - 1)
            if "x2" in dbg:
                t.dma("ACTQ", dbg_aps["d_comb"], comb[:, :, :])

        o_scope.close()
        t.barrier()

        y_scope = ExitStack()
        y = y_scope.enter_context(nc.sbuf_tensor("y_sb", [128, SP, D], F32))
        with ExitStack() as ph:
            e = ph.enter_context
            h_bf = e(nc.sbuf_tensor("h_bf", [128, NIC, 1024], BF16))
            w12s = e(nc.sbuf_tensor("w12s", [128, 2, D], F32))
            w1b = e(nc.sbuf_tensor("w1b", [128, 2, NCC, 128], BF16))
            w2b = e(nc.sbuf_tensor("w2b", [128, 2, NCC, 128], BF16))
            w3st = e(nc.sbuf_tensor("w3st", [128, 2, 512], F32))
            w3b = e(nc.sbuf_tensor("w3b", [128, 2, NIC, 512], BF16))
            s1b = e(nc.sbuf_tensor("s1b", [128, 2, 512], BF16))
            h1p = [e(nc.psum_tensor(f"h1p{i}", [128, 512], F32)) for i in range(2)]
            h2p = [e(nc.psum_tensor(f"h2p{i}", [128, 512], F32)) for i in range(2)]
            eop = [e(nc.psum_tensor(f"eop{i}", [128, 512], F32)) for i in range(2)]

            def load12(u, ic):
                rp = ic % 2
                t.dma("SPQ", w12s[:, 0, :], fw1[u, ic])
                t.copy("DVE",
                       w1b[:, rp, :, :].rearrange("p c x -> p (c x)"),
                       w12s[:, 0, :])
                t.dma("SPQ", w12s[:, 1, :], fw2[u, ic])
                t.copy("DVE",
                       w2b[:, rp, :, :].rearrange("p c x -> p (c x)"),
                       w12s[:, 1, :])

            def load3(u, cb, ic):
                t.dma("SPQ", w3st[:, ic % 2, :], fw3[u, cb, ic])
                t.copy("DVE", w3b[:, cb % 2, ic, :], w3st[:, ic % 2, :])

            load12(0, 0)
            for u in range(NU):
                for ic in range(NIC):
                    rp = ic % 2
                    if ic + 1 < NIC:
                        load12(u, ic + 1)
                    for hf in range(2):
                        for cc in range(NCC):
                            t.mm(h1p[hf][:, :], w1b[:, rp, cc, :],
                                 ln2t[:, cc, hf * 512:(hf + 1) * 512],
                                 start=(cc == 0), stop=(cc == NCC - 1),
                                 inc=(cc == NCC - 1))
                        for cc in range(NCC):
                            t.mm(h2p[hf][:, :], w2b[:, rp, cc, :],
                                 ln2t[:, cc, hf * 512:(hf + 1) * 512],
                                 start=(cc == 0), stop=(cc == NCC - 1),
                                 inc=(cc == NCC - 1))
                        t.act(s1b[:, hf, :], h1p[hf][:, :], AF.Silu)
                        t.tt("DVE", h_bf[:, ic, hf * 512:(hf + 1) * 512],
                             h2p[hf][:, :], s1b[:, hf, :], ALU.mult)
                for ic in range(NIC):
                    load3(u, 0, ic)
                for cb in range(4):
                    wp = cb % 2
                    if cb + 1 < 4:
                        for ic in range(NIC):
                            load3(u, cb + 1, ic)
                    elif u + 1 < NU:
                        load12(u + 1, 0)
                    for sa in range(SP):
                        p = eop[sa % 2]
                        for ic in range(NIC):
                            t.mm(p[:, :], h_bf[:, ic, sa * 128:(sa + 1) * 128],
                                 w3b[:, wp, ic, :], start=(ic == 0),
                                 stop=(ic == NIC - 1), inc=(ic == NIC - 1))
                        ysl = y[:, sa, cb * 512:(cb + 1) * 512]
                        if u == 0:
                            t.copy("DVE", ysl, p[:, :])
                        elif u == 1:
                            t.tt("DVE", ysl, p[:, :], ysl, ALU.add)
                        else:
                            t.stt("DVE", ysl, p[:, :], comb[:, sa, u - 2:u - 1],
                                  ysl, ALU.mult, ALU.add)

        t.barrier()

        with ExitStack() as p7:
            e7 = p7.enter_context
            x2r = e7(nc.sbuf_tensor("x2r", [128, 2, D], F32))
            outr = e7(nc.sbuf_tensor("outr", [128, 2, D], F32))
            t.dma("SPQ", x2r[:, 0, :], x2_d[0])
            t.dma("SPQ", x2r[:, 1, :], x2_d[1])
            for sa in range(SP):
                ysl = y[:, sa, :]
                ln_stats(ysl, sa % 2)
                o = outr[:, sa % 2, :]
                t.ts("DVE", o, ysl, mv[:, sa % 2, 0:1],
                     rstd[:, sa % 2:sa % 2 + 1], ALU.subtract, ALU.mult)
                t.tt("DVE", o, o, x2r[:, sa % 2, :], ALU.add)
                t.dma("ACTQ", out_d[sa], o)
                if sa + 2 < SP:
                    t.dma("SPQ", x2r[:, sa % 2, :], x2_d[sa + 2])
        y_scope.close()

        t.finish()
    return nc, dbg_aps



def _prep_shared(L0_attn_w_o, L0_attn_b_o, L0_ffn_gate_w,
                 L0_ffn_w1, L0_ffn_w2, L0_ffn_w3,
                 L0_ffn_sw1, L0_ffn_sw2, L0_ffn_sw3):
    wo_p = np.ascontiguousarray(
        L0_attn_w_o.reshape(NCC, 128, 4, 512).transpose(2, 1, 0, 3))
    bo_r = np.ascontiguousarray(L0_attn_b_o.reshape(1, D))
    gwt_p = np.ascontiguousarray(
        L0_ffn_gate_w.T.reshape(NCC, 128, 8).transpose(1, 0, 2))

    def prep_w12(w_sh, w_ex):
        sh = w_sh.T.reshape(NCC, 128, 2, NIC, 128).transpose(2, 3, 1, 0, 4)
        ex = np.stack([w_ex[ei].T.reshape(NCC, 128, NIC, 128).transpose(2, 1, 0, 3)
                       for ei in range(E)])
        return np.ascontiguousarray(
            np.concatenate([sh, ex], axis=0).reshape(NU, NIC, 128, D))

    fw1 = prep_w12(L0_ffn_sw1, L0_ffn_w1)
    fw2 = prep_w12(L0_ffn_sw2, L0_ffn_w2)

    sh3 = L0_ffn_sw3.T.reshape(2, NIC, 128, 4, 512).transpose(0, 3, 1, 2, 4)
    ex3 = np.stack([L0_ffn_w3[ei].T.reshape(NIC, 128, 4, 512).transpose(2, 0, 1, 3)
                    for ei in range(E)])
    fw3 = np.ascontiguousarray(np.concatenate([sh3, ex3], axis=0))

    masks = np.zeros((128, 4, 512), np.float32)
    for j in range(4):
        for ki in range(128):
            masks[ki, j, :128 * j + ki] = NEG
    masks = masks.astype(ml_dtypes.bfloat16)
    ident = np.eye(128, dtype=ml_dtypes.bfloat16)
    ones_b = np.ones((128, 128), ml_dtypes.bfloat16)
    ones_f = np.ones((128, 128), np.float32)
    return dict(wo_p=wo_p, bo_r=bo_r, gwt_p=gwt_p, fw1=fw1, fw2=fw2, fw3=fw3,
                masks=masks, ident=ident, ones_b=ones_b, ones_f=ones_f)


def _prep_core(c, x, w_q, b_q, w_k, b_k, w_v, b_v):
    own = np.concatenate([x[c * CH:(c + 1) * CH],
                          x[(15 - c) * CH:(16 - c) * CH]], axis=0)
    x_own = np.ascontiguousarray(own.reshape(SP, 128, D))
    cols = slice(256 * c, 256 * (c + 1))

    def wprep(w):
        return np.ascontiguousarray(
            w[:, cols].reshape(NCC, 128, 256).transpose(1, 0, 2))

    bq = b_q[cols].reshape(2, 128).T
    bk = b_k[cols].reshape(2, 128).T
    bqk = np.ascontiguousarray(
        np.stack([bq[:, 0], bq[:, 1], bk[:, 0], bk[:, 1]], axis=1))
    bv_m = np.ascontiguousarray(b_v[cols].reshape(1, 256))
    oidx = np.zeros((128, 32), np.int32)
    q1 = 15 - c
    piece = 0 if q1 < 12 else 1
    within = q1 - 8 - 4 * piece
    for cc in range(NCC):
        for p in range(128):
            oidx[p, cc * 2 + 0] = (cc * 128 + p) * 8 + c
            oidx[p, cc * 2 + 1] = piece * 8192 + (cc * 128 + p) * 4 + within
    return dict(x_own=x_own, wq_m=wprep(w_q), wk_m=wprep(w_k), wv_m=wprep(w_v),
                bqk_m=bqk, bv_m=bv_m, oidx=oidx)


_BUILD_CACHE = {}


def kernel(L0_attn_query, L0_attn_w_q, L0_attn_b_q, L0_attn_w_k, L0_attn_b_k,
           L0_attn_w_v, L0_attn_b_v, L0_attn_w_o, L0_attn_b_o, L0_ffn_gate_w,
           L0_ffn_w1, L0_ffn_w2, L0_ffn_w3, L0_ffn_sw1, L0_ffn_sw2, L0_ffn_sw3,
           _debug_outputs=(), _return_raw=False):
    x = np.asarray(L0_attn_query, np.float32).reshape(T, D)
    shared = _prep_shared(np.asarray(L0_attn_w_o, np.float32),
                          np.asarray(L0_attn_b_o, np.float32),
                          np.asarray(L0_ffn_gate_w, np.float32),
                          np.asarray(L0_ffn_w1, np.float32),
                          np.asarray(L0_ffn_w2, np.float32),
                          np.asarray(L0_ffn_w3, np.float32),
                          np.asarray(L0_ffn_sw1, np.float32),
                          np.asarray(L0_ffn_sw2, np.float32),
                          np.asarray(L0_ffn_sw3, np.float32))
    in_maps = []
    for c in range(NCORE):
        m = dict(shared)
        m.update(_prep_core(c, x,
                            np.asarray(L0_attn_w_q, np.float32),
                            np.asarray(L0_attn_b_q, np.float32),
                            np.asarray(L0_attn_w_k, np.float32),
                            np.asarray(L0_attn_b_k, np.float32),
                            np.asarray(L0_attn_w_v, np.float32),
                            np.asarray(L0_attn_b_v, np.float32)))
        in_maps.append(m)

    key = tuple(sorted(_debug_outputs))
    if key not in _BUILD_CACHE:
        _BUILD_CACHE[key] = build(_debug_outputs)
    nc, dbg_aps = _BUILD_CACHE[key]

    res = run_bass_kernel_spmd(nc, in_maps, core_ids=list(range(NCORE)))
    if _return_raw:
        return res
    full = np.empty((T, D), np.float32)
    for c in range(NCORE):
        o = res.results[c]["out"].reshape(SP * 128, D)
        full[c * CH:(c + 1) * CH] = o[:CH]
        full[(15 - c) * CH:(16 - c) * CH] = o[CH:]
    return full.reshape(1, T, D)


# version 14
# speedup vs baseline: 1.1416x; 1.1416x over previous
import numpy as np
import ml_dtypes
from contextlib import ExitStack

import concourse.bass as bass
import concourse.mybir as mybir
from concourse.bass_utils import run_bass_kernel_spmd

F32 = mybir.dt.float32
BF16 = mybir.dt.bfloat16
I32 = mybir.dt.int32
ALU = mybir.AluOpType
AF = mybir.ActivationFunctionType
AX = mybir.AxisListType

T = 8192
D = 2048
NH = 16
HS = 128
NCORE = 8
CH = 512
NCHK = 16
SP = 8
E = 8
IM = 1408
NIC = 11
ISH = 2816
NU = 10
NCC = 16
EPS = 1e-5
ISR = 0.08838834764831845
NEG = -1.0e9


def tb_of_chunk(c):
    return 2 * c if c < 8 else 2 * (15 - c) + 1



def _rect(ap):
    t = ap.tensor
    name = t.name
    try:
        off = int(ap.offset)
    except Exception:
        return (name, 0, 1 << 20, 0, 1 << 60)
    dims = list(ap.ap)
    if not dims:
        return (name, 0, 1 << 20, off, off + 1)
    space = str(ap.space)
    asz = mybir.dt.size(ap.dtype)
    if "DRAM" in space:
        lo = off
        hi = off
        for st, cnt in dims:
            if st >= 0:
                hi += st * (cnt - 1)
            else:
                lo += st * (cnt - 1)
        return (name, 0, 1 << 20, lo * asz, (hi + 1) * asz)
    row = 1
    for s in t.shape[1:]:
        row *= s
    tsz = mybir.dt.size(t.dtype)
    if tsz != asz:
        row = row * tsz // asz
    p0 = off // row
    f_lo = off % row
    f_hi = f_lo
    nump = 1
    for st, cnt in dims:
        if st != 0 and st % row == 0:
            nump = max(nump, (cnt - 1) * (st // row) + 1)
        else:
            if st >= 0:
                f_hi += st * (cnt - 1)
            else:
                f_lo += st * (cnt - 1)
    if "PSUM" in space:
        return (name, 0, 128, 0, 1 << 30)
    return (name, p0, p0 + nump, f_lo * asz, (f_hi + 1) * asz)


def _overlap(a, b):
    return (a[0] == b[0]) and not (
        a[2] <= b[1] or b[2] <= a[1] or a[4] <= b[3] or b[4] <= a[3]
    )


def _contains(a, b):
    return (a[0] == b[0]) and a[1] <= b[1] and a[2] >= b[2] and a[3] <= b[3] and a[4] >= b[4]


class Trk:
    def __init__(self, nc):
        self.nc = nc
        self.streams = {}
        self.cnt = {}
        self.waited = {}
        self.writes = {}
        self.reads = {}
        for name, eng, ekey, amt in [
            ("TE", nc.tensor, "te", 1),
            ("DVE", nc.vector, "dve", 1),
            ("ACT", nc.scalar, "act", 1),
            ("POOL", nc.gpsimd, "pool", 1),
            ("SPQ", nc.sync, "sp", 16),
            ("ACTQ", nc.scalar, "act", 16),
            ("POOLQ", nc.gpsimd, "pool", 16),
        ]:
            sem = nc.alloc_semaphore(f"trk_{name}")
            self.streams[name] = (eng, ekey, sem, amt)
            self.cnt[name] = 0

    def _deps_for(self, rect, want_writes, want_reads):
        deps = []
        if want_writes:
            for r, s, v in self.writes.get(rect[0], ()):
                if _overlap(rect, r):
                    deps.append((s, v))
        if want_reads:
            for r, s, v in self.reads.get(rect[0], ()):
                if _overlap(rect, r):
                    deps.append((s, v))
        return deps

    def _emit_waits(self, stream, deps):
        eng, ekey, ssem, _ = self.streams[stream]
        best = {}
        selfmax = 0
        for s, v in deps:
            if s == stream:
                if stream not in ("TE", "SPQ", "ACTQ", "POOLQ"):
                    selfmax = max(selfmax, v)
                continue
            _, dekey, dsem, _ = self.streams[s]
            if s == "TE" and ekey == "te":
                continue
            if dekey == ekey and dsem.name == ssem.name:
                continue
            key = (ekey, dsem.name)
            if self.waited.get(key, 0) >= v:
                continue
            cur = best.get(dsem.name)
            if cur is None or v > cur[1]:
                best[dsem.name] = (dsem, v, key)
        for dsem, v, key in best.values():
            eng.wait_ge(dsem, v)
            self.waited[key] = v
        if selfmax > 0:
            key = (ekey + "#self", ssem.name)
            if self.waited.get(key, 0) < selfmax:
                eng.wait_ge(ssem, selfmax)
                self.waited[key] = selfmax

    def _record(self, rect, stream, value, is_write):
        name = rect[0]
        if is_write:
            wl = self.writes.setdefault(name, [])
            wl[:] = [e for e in wl if not _contains(rect, e[0])]
            wl.append((rect, stream, value))
            rl = self.reads.setdefault(name, [])
            rl[:] = [e for e in rl if not _contains(rect, e[0])]
        else:
            rl = self.reads.setdefault(name, [])
            rl[:] = [e for e in rl
                     if not (_contains(rect, e[0]) and e[1] == stream)]
            rl.append((rect, stream, value))

    def op(self, stream, fn, reads=(), writes=(), inc=True):
        eng, ekey, sem, amt = self.streams[stream]
        rrects = [_rect(a) for a in reads]
        wrects = [_rect(a) for a in writes]
        deps = []
        for r in rrects:
            deps += self._deps_for(r, True, False)
        for r in wrects:
            deps += self._deps_for(r, True, True)
        self._emit_waits(stream, deps)
        inst = fn(eng)
        if inc:
            inst.then_inc(sem, amt)
            self.cnt[stream] += amt
        val = self.cnt[stream] + (0 if inc else amt)
        for r in rrects:
            self._record(r, stream, val, False)
        for r in wrects:
            self._record(r, stream, val, True)
        return inst

    def dma(self, queue, out, in_, **kw):
        return self.op(queue, lambda e: e.dma_start(out=out, in_=in_, **kw),
                       reads=[in_], writes=[out])

    def mm(self, out, lhsT, rhs, start=True, stop=True, inc=True, **kw):
        return self.op("TE",
                       lambda e: e.matmul(out, lhsT, rhs, start=start, stop=stop, **kw),
                       reads=[lhsT, rhs], writes=[out], inc=inc)

    def tr(self, out, in_, ident, inc=True):
        return self.op("TE", lambda e: e.transpose(out, in_, ident),
                       reads=[in_, ident], writes=[out], inc=inc)

    def act(self, out, in_, func, bias=0.0, scale=1.0, accum_out=None):
        reads = [in_]
        writes = [out]
        if hasattr(bias, "tensor"):
            reads.append(bias)
        if accum_out is not None:
            writes = [out, accum_out]
            return self.op("ACT",
                           lambda e: e.activation(out, in_, func, bias=bias,
                                                  scale=scale, accum_out=accum_out),
                           reads=reads, writes=writes)
        return self.op("ACT",
                       lambda e: e.activation(out, in_, func, bias=bias, scale=scale),
                       reads=reads, writes=writes)

    def copy(self, stream, out, in_):
        return self.op(stream, lambda e: e.tensor_copy(out, in_),
                       reads=[in_], writes=[out])

    def tt(self, stream, out, in0, in1, alu):
        return self.op(stream, lambda e: e.tensor_tensor(out, in0, in1, alu),
                       reads=[in0, in1], writes=[out])

    def ts(self, stream, out, in0, s1, s2, op0, op1=None):
        reads = [in0]
        if hasattr(s1, "tensor"):
            reads.append(s1)
        if s2 is not None and hasattr(s2, "tensor"):
            reads.append(s2)
        if op1 is None:
            return self.op(stream, lambda e: e.tensor_scalar(out, in0, s1, s2, op0),
                           reads=reads, writes=[out])
        return self.op(stream, lambda e: e.tensor_scalar(out, in0, s1, s2, op0, op1),
                       reads=reads, writes=[out])

    def stt(self, stream, out, in0, scalar, in1, op0, op1):
        reads = [in0, in1]
        if hasattr(scalar, "tensor"):
            reads.append(scalar)
        return self.op(stream,
                       lambda e: e.scalar_tensor_tensor(out, in0, scalar, in1, op0, op1),
                       reads=reads, writes=[out])

    def red(self, stream, out, in_, alu):
        return self.op(stream, lambda e: e.tensor_reduce(out, in_, AX.X, alu),
                       reads=[in_], writes=[out])

    def recip(self, out, in_):
        return self.op("DVE", lambda e: e.reciprocal(out, in_),
                       reads=[in_], writes=[out])

    def bn(self, out, in_, nchunks):
        n = in_.shape[-1] // nchunks
        for i in range(nchunks):
            sl = in_[:, i * n:(i + 1) * n]
            self.op("DVE", lambda e, i=i, sl=sl: e.bn_stats(
                self._bn_buf[:, i, :], sl),
                reads=[sl], writes=[self._bn_buf[:, i, :]])
        self.op("DVE", lambda e: e.bn_aggr(out, self._bn_buf[:, 0:nchunks, :]),
                reads=[self._bn_buf[:, 0:nchunks, :]], writes=[out])

    def collective(self, kind, aluop, ins, outs, replica_groups):
        sem = self.nc.alloc_semaphore(f"cc_{len(self.streams)}")
        eng, ekey, _, _ = self.streams["POOL"]
        deps = []
        for a in ins:
            deps += self._deps_for(_rect(a), True, False)
        for a in outs:
            deps += self._deps_for(_rect(a), True, True)
        self._emit_waits("POOL", deps)
        inst = eng.collective_compute(kind, aluop, replica_groups=replica_groups,
                                      ins=ins, outs=outs)
        inst.then_inc(sem)
        sname = f"CC{len(self.streams)}"
        self.streams[sname] = (eng, "pool", sem, 1)
        self.cnt[sname] = 1
        for a in outs:
            self._record(_rect(a), sname, 1, True)
        for a in ins:
            self._record(_rect(a), sname, 1, False)
        return inst

    def indirect(self, out, in_, idx):
        from concourse.bass import IndirectOffsetOnAxis
        return self.op("POOLQ",
                       lambda e: e.indirect_dma_start(
                           out, None, in_, IndirectOffsetOnAxis(idx, 0)),
                       reads=[in_, idx], writes=[out])

    def barrier(self):
        snap = [(sname, tup[2], self.cnt.get(sname, 0))
                for sname, tup in self.streams.items()
                if self.cnt.get(sname, 0) > 0 and not sname.startswith("CC")]
        for wname in ("TE", "DVE", "ACT", "POOL", "SPQ"):
            weng, wekey, _, _ = self.streams[wname]
            for sname, sem, v in snap:
                key = (wekey, sem.name)
                if self.waited.get(key, 0) < v:
                    weng.wait_ge(sem, v)
                    self.waited[key] = v

    def finish(self):
        eng, ekey, _, _ = self.streams["POOL"]
        for sname, (seng, sekey, sem, amt) in list(self.streams.items()):
            v = self.cnt.get(sname, 0)
            if v > 0:
                key = (ekey, sem.name)
                if self.waited.get(key, 0) < v:
                    eng.wait_ge(sem, v)
                    self.waited[key] = v



def build(debug_outputs=()):
    nc = bass.Bass("TRN2", target_bir_lowering=False)
    dbg = set(debug_outputs)

    x_own = nc.dram_tensor("x_own", [SP, 128, D], F32, kind="ExternalInput").ap()
    wq_m = nc.dram_tensor("wq_m", [128, NCC, 256], F32, kind="ExternalInput").ap()
    wk_m = nc.dram_tensor("wk_m", [128, NCC, 256], F32, kind="ExternalInput").ap()
    wv_m = nc.dram_tensor("wv_m", [128, NCC, 256], F32, kind="ExternalInput").ap()
    bqk_m = nc.dram_tensor("bqk_m", [128, 4], F32, kind="ExternalInput").ap()
    bv_m = nc.dram_tensor("bv_m", [1, 256], F32, kind="ExternalInput").ap()
    bo_r = nc.dram_tensor("bo_r", [1, D], F32, kind="ExternalInput").ap()
    wo_p = nc.dram_tensor("wo_p", [4, 128, NCC, 512], F32, kind="ExternalInput").ap()
    gwt_p = nc.dram_tensor("gwt_p", [128, NCC, 8], F32, kind="ExternalInput").ap()
    fw1 = nc.dram_tensor("fw1", [NU, NIC, 128, D], F32, kind="ExternalInput").ap()
    fw2 = nc.dram_tensor("fw2", [NU, NIC, 128, D], F32, kind="ExternalInput").ap()
    fw3 = nc.dram_tensor("fw3", [NU, 4, NIC, 128, 512], F32, kind="ExternalInput").ap()
    masks_d = nc.dram_tensor("masks", [128, 4, 512], BF16, kind="ExternalInput").ap()
    ident_d = nc.dram_tensor("ident", [128, 128], BF16, kind="ExternalInput").ap()
    ones_b_d = nc.dram_tensor("ones_b", [128, 128], BF16, kind="ExternalInput").ap()
    ones_f_d = nc.dram_tensor("ones_f", [128, 128], F32, kind="ExternalInput").ap()
    oidx_d = nc.dram_tensor("oidx", [128, 32], I32, kind="ExternalInput").ap()

    out_d = nc.dram_tensor("out", [SP, 128, D], F32, kind="ExternalOutput").ap()

    ln1_own = nc.dram_tensor("ln1_own", [2, D, 512], BF16, kind="Internal").ap()
    ln1_alla = nc.dram_tensor("ln1_alla", [NCORE * D, 512], BF16, kind="Internal", addr_space="Shared").ap()
    ln1_allb = nc.dram_tensor("ln1_allb", [NCORE * D, 512], BF16, kind="Internal", addr_space="Shared").ap()
    ccwarm_i = nc.dram_tensor("ccwarm_i", [16, 16], BF16, kind="Internal").ap()
    ccwarm_o = nc.dram_tensor("ccwarm_o", [128, 16], BF16, kind="Internal", addr_space="Shared").ap()
    x2_d = nc.dram_tensor("x2_d", [SP, 128, D], F32, kind="Internal").ap()

    o_own_a = nc.dram_tensor("o_own_a", [256, T // 2], BF16, kind="Internal").ap()
    o_own_b = nc.dram_tensor("o_own_b", [256, T // 4], BF16, kind="Internal").ap()
    o_own_c = nc.dram_tensor("o_own_c", [256, T // 4], BF16, kind="Internal").ap()
    o_all_a = nc.dram_tensor("o_all_a", [D, T // 2], BF16, kind="Internal", addr_space="Shared").ap()
    o_all_bc = nc.dram_tensor("o_all_bc", [2, D, T // 4], BF16, kind="Internal", addr_space="Shared").ap()

    dbg_aps = {}
    if "ln1_all" in dbg:
        dbg_aps["d_ln1"] = nc.dram_tensor("d_ln1", [NCORE * D, 1024], BF16,
                                          kind="ExternalOutput").ap()
    if "kqv" in dbg:
        dbg_aps["d_k"] = nc.dram_tensor("d_k", [128, 2, NCHK, 512], BF16,
                                        kind="ExternalOutput").ap()
        dbg_aps["d_q"] = nc.dram_tensor("d_q", [128, 2, NCHK, 512], BF16,
                                        kind="ExternalOutput").ap()
        dbg_aps["d_v"] = nc.dram_tensor("d_v", [128, 64, 256], BF16,
                                        kind="ExternalOutput").ap()
    if "o_all" in dbg:
        dbg_aps["d_o"] = nc.dram_tensor("d_o", [D, T], BF16,
                                        kind="ExternalOutput").ap()
    if "x2" in dbg:
        dbg_aps["d_x2"] = nc.dram_tensor("d_x2", [SP, 128, D], F32,
                                         kind="ExternalOutput").ap()
        dbg_aps["d_comb"] = nc.dram_tensor("d_comb", [128, SP, 8], F32,
                                           kind="ExternalOutput").ap()

    t = Trk(nc)
    RG = [list(range(NCORE))]

    with ExitStack() as top:
        et = top.enter_context
        ident = et(nc.sbuf_tensor("ident_sb", [128, 128], BF16))
        ones_b = et(nc.sbuf_tensor("ones_b_sb", [128, 128], BF16))
        ones_f = et(nc.sbuf_tensor("ones_f_sb", [128, 128], F32))
        bqk = et(nc.sbuf_tensor("bqk_sb", [128, 4], F32))
        bv_row = et(nc.sbuf_tensor("bv_row", [1, 256], F32))
        bv_bc = et(nc.sbuf_tensor("bv_bc", [128, 256], F32))
        gwt = et(nc.sbuf_tensor("gwt_sb", [128, NCC, 8], F32))
        gw_hi = et(nc.sbuf_tensor("gw_hi", [128, NCC, 8], BF16))
        gw_lo = et(nc.sbuf_tensor("gw_lo", [128, NCC, 8], BF16))
        shavg = et(nc.sbuf_tensor("shavg", [128, 1], F32))
        sh_tmp = et(nc.sbuf_tensor("sh_tmp", [128, 2], F32))
        sh_row = et(nc.sbuf_tensor("sh_row", [1, 2], F32))
        oidx = et(nc.sbuf_tensor("oidx_sb", [128, 32], I32))
        comb = et(nc.sbuf_tensor("comb_sb", [128, SP, 8], F32))
        bnb = et(nc.sbuf_tensor("bn_buf", [128, 4, 6], F32))
        mv = et(nc.sbuf_tensor("mv_sb", [128, 2, 2], F32))
        rstd = et(nc.sbuf_tensor("rstd_sb", [128, 2], F32))
        eps_t = et(nc.sbuf_tensor("eps_t", [128, 1], F32))
        t._bn_buf = bnb
        t.op("DVE", lambda e: e.memset(eps_t[:, :], EPS), writes=[eps_t[:, :]])

        o_scope = ExitStack()
        oT = o_scope.enter_context(nc.sbuf_tensor("oT_sb", [128, NCC, 1024], BF16, side="right"))
        w_scope = ExitStack()
        ew = w_scope.enter_context
        wst = ew(nc.sbuf_tensor("w_stage", [128, 8, 256], F32))
        wqb = ew(nc.sbuf_tensor("wq_b", [128, NCC, 256], BF16))
        wkb = ew(nc.sbuf_tensor("wk_b", [128, NCC, 256], BF16))
        wvb = ew(nc.sbuf_tensor("wv_b", [128, NCC, 256], BF16))

        t.dma("ACTQ", ident[:, :], ident_d)
        t.dma("ACTQ", ones_b[:, :], ones_b_d)
        t.dma("ACTQ", ones_f[:, :], ones_f_d)
        t.dma("ACTQ", bqk[:, :], bqk_m)
        t.dma("ACTQ", bv_row[:, :], bv_m)
        t.dma("ACTQ", gwt[:, :, :], gwt_p)
        t.dma("ACTQ", oidx[:, :], oidx_d)
        t.collective("AllGather", ALU.bypass, [ccwarm_i], [ccwarm_o], RG)


        def warm(pt):
            ap = pt[:, :]
            if ap.dtype == BF16:
                ap = ap.bitcast(F32)
            t.op("DVE", lambda e, ap=ap: e.memset(ap, 0.0), writes=[ap])

        def ln_stats(src, slot):
            t.bn(mv[:, slot, :], src, 4)
            t.act(rstd[:, slot:slot + 1], mv[:, slot, 1:2], AF.Ln, bias=eps_t[:, :])
            t.act(rstd[:, slot:slot + 1], rstd[:, slot:slot + 1], AF.Exp, scale=-0.5)

        with ExitStack() as ph:
            e = ph.enter_context
            x_ring = e(nc.sbuf_tensor("x_ring", [128, 2, D], F32))
            ln1_tok = e(nc.sbuf_tensor("ln1_tok", [128, SP, D], BF16))
            tr_st = e(nc.sbuf_tensor("tr_st", [128, 2, 1024], BF16))
            cps = e(nc.psum_tensor("cps", [128, 512], F32))
            trp = [e(nc.psum_tensor(f"trp{i}", [128, 1024], BF16)) for i in range(2)]
            for pt in [cps] + trp:
                warm(pt)

            t.mm(cps[:, 0:256], ones_f[0:1, :], bv_row[0:1, :])
            t.copy("DVE", bv_bc[:, :], cps[:, 0:256])
            t.stt("DVE", sh_tmp[:, :], bqk[:, 0:2], -ISR, bqk[:, 2:4],
                  ALU.mult, ALU.mult)
            t.mm(cps[0:1, 0:2], ones_f[:, 0:1], sh_tmp[:, :])
            t.copy("DVE", sh_row[0:1, :], cps[0:1, 0:2])
            t.red("DVE", sh_row[0:1, 0:1], sh_row[0:1, :], ALU.add)
            t.ts("DVE", sh_row[0:1, 0:1], sh_row[0:1, 0:1], 0.5, None, ALU.mult)
            t.mm(cps[:, 0:1], ones_f[0:1, :], sh_row[0:1, 0:1])
            t.copy("DVE", shavg[:, :], cps[:, 0:1])
            t.copy("DVE", gw_hi[:, :, :], gwt[:, :, :])
            t.tt("DVE", gw_lo[:, :, :], gwt[:, :, :], gw_hi[:, :, :], ALU.subtract)

            for s in range(SP):
                xs = x_ring[:, s % 2, :]
                t.dma("SPQ", xs, x_own[s])
                ln_stats(xs, s % 2)
                t.ts("DVE", ln1_tok[:, s, :], xs,
                     mv[:, s % 2, 0:1], rstd[:, s % 2:s % 2 + 1],
                     ALU.subtract, ALU.mult)
            for h in range(2):
                for cc in range(NCC):
                    pb = trp[cc % 2]
                    for si in range(4):
                        sa = h * 4 + si
                        t.tr(pb[:, si * 128:(si + 1) * 128],
                             ln1_tok[:, sa, cc * 128:(cc + 1) * 128], ident[:, :],
                             inc=(si == 3))
                    st = tr_st[:, cc % 2, 0:512]
                    t.copy("DVE", st, pb[:, 0:512])
                    t.dma("ACTQ", ln1_own[h, cc * 128:(cc + 1) * 128, :], st)
                t.collective("AllGather", ALU.bypass, [ln1_own[h]],
                             [ln1_alla if h == 0 else ln1_allb], RG)
            for wsrc, wdst in ((wq_m, wqb), (wk_m, wkb), (wv_m, wvb)):
                for hh in range(2):
                    t.dma("SPQ", wst[:, :, :], wsrc[:, hh * 8:(hh + 1) * 8, :])
                    t.copy("DVE", wdst[:, hh * 8:(hh + 1) * 8, :], wst[:, :, :])

        t.barrier()

        kqv_scope = ExitStack()
        e = kqv_scope.enter_context
        masks = e(nc.sbuf_tensor("masks_sb", [128, 4, 512], BF16))
        k_res = e(nc.sbuf_tensor("k_res", [128, 2, NCHK, 512], BF16))
        q_res = e(nc.sbuf_tensor("q_res", [128, 2, NCHK, 512], BF16))
        v_res = e(nc.sbuf_tensor("v_res", [128, 64, 256], BF16))

        t.dma("ACTQ", masks[:, :, :], masks_d)
        with ExitStack() as ph:
            e = ph.enter_context
            lnr = e(nc.sbuf_tensor("lnall_ring", [128, 2, NCC, 512], BF16))
            kp = [e(nc.psum_tensor(f"kp{i}", [128, 512], F32)) for i in range(2)]
            qp = [e(nc.psum_tensor(f"qp{i}", [128, 512], F32)) for i in range(2)]
            vp = [e(nc.psum_tensor(f"vp{i}", [128, 512], F32)) for i in range(2)]
            for pt in kp + qp + vp:
                warm(pt)

            def lnall_block(tb):
                src = ln1_alla if tb % 2 == 0 else ln1_allb
                r = tb // 2
                return src.rearrange("(r cc p) q -> p r cc q", p=128, cc=NCC)[:, r]

            tb_order = list(range(0, NCHK, 2)) + list(range(1, NCHK, 2))
            t.dma("SPQ", lnr[:, 0, :, :], lnall_block(tb_order[0]))
            for ti in range(NCHK):
                tb = tb_order[ti]
                if ti + 1 < NCHK:
                    t.dma("SPQ", lnr[:, (ti + 1) % 2, :, :],
                          lnall_block(tb_order[ti + 1]))
                ln = lnr[:, ti % 2, :, :]
                for hp in range(2):
                    for cc in range(NCC):
                        t.mm(kp[hp][:, :], wkb[:, cc, hp * 128:(hp + 1) * 128],
                             ln[:, cc, :], start=(cc == 0), stop=(cc == NCC - 1),
                             inc=(cc == NCC - 1))
                    t.act(k_res[:, hp, tb, :], kp[hp][:, :], AF.Identity,
                          bias=bqk[:, 2 + hp:3 + hp])
                    for cc in range(NCC):
                        t.mm(qp[hp][:, :], wqb[:, cc, hp * 128:(hp + 1) * 128],
                             ln[:, cc, :], start=(cc == 0), stop=(cc == NCC - 1),
                             inc=(cc == NCC - 1))
                    t.act(q_res[:, hp, tb, :], qp[hp][:, :], AF.Identity,
                          bias=bqk[:, hp:hp + 1])
                for j in range(4):
                    p = vp[j % 2]
                    for cc in range(NCC):
                        t.mm(p[:, 0:256], ln[:, cc, j * 128:(j + 1) * 128],
                             wvb[:, cc, :], start=(cc == 0), stop=(cc == NCC - 1),
                             inc=(cc == NCC - 1))
                    t.tt("DVE", v_res[:, tb * 4 + j, :], p[:, 0:256], bv_bc[:, :],
                         ALU.add)
            if "kqv" in dbg:
                t.dma("ACTQ", dbg_aps["d_k"], k_res[:, :, :, :])
                t.dma("ACTQ", dbg_aps["d_q"], q_res[:, :, :, :])
                t.dma("ACTQ", dbg_aps["d_v"], v_res[:, :, :])

        t.barrier()

        with ExitStack() as ph:
            e = ph.enter_context
            p_ring = e(nc.sbuf_tensor("p_ring", [128, 4, 1024], BF16))
            o_st = e(nc.sbuf_tensor("o_st", [128, 2, 2, 512], BF16))
            rcp = e(nc.sbuf_tensor("rcp_sb", [1, 2, 512], F32))
            r_sb = e(nc.sbuf_tensor("r_sb", [128, 2, 512], F32))
            lg = [e(nc.psum_tensor(f"lg{i}", [128, 1024], F32)) for i in range(2)]
            ops = [e(nc.psum_tensor(f"op{i}", [128, 512], F32)) for i in range(2)]
            sps = [e(nc.psum_tensor(f"sp{i}", [128, 512], F32)) for i in range(2)]
            for pt in lg + ops + sps:
                warm(pt)

            o_flat_a = o_all_a.rearrange("r (k c) -> (r k) c", c=512)
            o_flat_bc = o_all_bc.rearrange("h r (k c) -> (h r k) c", c=512)

            def norm_store(qb):
                for hp in range(2):
                    t.recip(rcp[0:1, hp, :], sps[hp][0:1, :])
                    t.mm(sps[hp][:, :], ones_f[0:1, :], rcp[0:1, hp, :])
                    t.copy("DVE", r_sb[:, hp, :], sps[hp][:, :])
                    t.tt("DVE", o_st[:, qb % 2, hp, :], ops[hp][:, :],
                         r_sb[:, hp, :], ALU.mult)
                if qb < 8:
                    half, cq = o_own_a, qb
                elif qb < 12:
                    half, cq = o_own_b, qb - 8
                else:
                    half, cq = o_own_c, qb - 12
                t.dma("ACTQ",
                      half.rearrange("(h p) q -> p h q", p=128)[
                          :, :, cq * 512:(cq + 1) * 512],
                      o_st[:, qb % 2, :, :])

            def sums_av(n, qb_n, tc, j, first, last):
                for hp in range(2):
                    phalf = p_ring[:, n % 4, hp * 512:(hp + 1) * 512]
                    t.mm(sps[hp][0:64, :], ones_b[:, 0:64], phalf,
                         start=first, stop=last, inc=False)
                    t.mm(ops[hp][:, :],
                         v_res[:, tc * 4 + j, hp * 128:(hp + 1) * 128],
                         phalf, start=first, stop=last, inc=(hp == 1))

            tiles = []
            for qb in range(NCHK):
                for c in range(qb + 1):
                    for j in range(4):
                        tiles.append((qb, c, j, c == 0 and j == 0,
                                      c == qb and j == 3))
            pend = None
            pend_norm = None
            gn = 0
            for (qb, c, j, first, last) in tiles:
                g = lg[gn % 2]
                ps = p_ring[:, gn % 4, :]
                tc = tb_of_chunk(c)
                tq = tb_of_chunk(qb)
                t.mm(g[:, 0:512], k_res[:, 0, tc, j * 128:(j + 1) * 128],
                     q_res[:, 0, tq, :], inc=False)
                t.mm(g[:, 512:1024], k_res[:, 1, tc, j * 128:(j + 1) * 128],
                     q_res[:, 1, tq, :], inc=True)
                if c == qb:
                    gv = g[:, :].rearrange("p (h q) -> p h q", h=2)
                    t.tt("DVE", gv, gv,
                         masks[:, j:j + 1, :].broadcast_to([128, 2, 512]),
                         ALU.add)
                t.act(ps, g[:, :], AF.Exp, bias=shavg[:, :], scale=ISR)
                if pend_norm is not None:
                    norm_store(pend_norm)
                    if pend_norm == 7:
                        t.collective("AllGather", ALU.bypass,
                                     [o_own_a], [o_all_a], RG)
                        for cc in range(NCC):
                            t.indirect(oT[:, cc, 0:512], o_flat_a,
                                       oidx[:, cc * 2:cc * 2 + 1])
                    elif pend_norm == 11:
                        t.collective("AllGather", ALU.bypass,
                                     [o_own_b], [o_all_bc[0]], RG)
                    pend_norm = None
                if pend is not None:
                    sums_av(pend[5], pend[0], tb_of_chunk(pend[1]), pend[2],
                            pend[3], pend[4])
                    if pend[4]:
                        pend_norm = pend[0]
                pend = (qb, c, j, first, last, gn)
                gn += 1
            sums_av(pend[5], pend[0], tb_of_chunk(pend[1]), pend[2],
                    pend[3], pend[4])
            norm_store(pend[0])
            t.collective("AllGather", ALU.bypass, [o_own_c], [o_all_bc[1]], RG)
            for cc in range(NCC):
                t.indirect(oT[:, cc, 512:1024], o_flat_bc,
                           oidx[:, cc * 2 + 1:cc * 2 + 2])
            if "o_all" in dbg:
                t.dma("ACTQ", dbg_aps["d_o"][:, 0:T // 2], o_all_a)
                t.dma("ACTQ", dbg_aps["d_o"][:, T // 2:3 * T // 4], o_all_bc[0])
                t.dma("ACTQ", dbg_aps["d_o"][:, 3 * T // 4:T], o_all_bc[1])

        kqv_scope.close()
        w_scope.close()
        t.barrier()

        ln2t = top.enter_context(nc.sbuf_tensor("ln2t", [128, NCC, 1024], BF16))
        with ExitStack() as ph:
            e = ph.enter_context
            attn_q = e(nc.sbuf_tensor("attn_q", [128, 2, D], F32))
            wo_st = e(nc.sbuf_tensor("wo_st", [128, 2, 4, 512], F32))
            wo_all = e(nc.sbuf_tensor("wo_all", [128, 4, NCC, 512], BF16))
            bo_bc = e(nc.sbuf_tensor("bo_bc", [128, D], F32))
            x_r2 = e(nc.sbuf_tensor("x_r2", [128, 2, D], F32))
            hi_r = e(nc.sbuf_tensor("hi_r", [128, 2, D], BF16))
            lo_s = e(nc.sbuf_tensor("lo_s", [128, 2, NCC, 128], BF16))
            g_t = e(nc.sbuf_tensor("g_t", [128, 8, 8], F32))
            nmr_t = e(nc.sbuf_tensor("nmr_t", [128, 2], F32))
            opp = [e(nc.psum_tensor(f"opp{i}", [128, 512], F32)) for i in range(2)]
            tp = e(nc.psum_tensor("tp", [128, 1024], BF16))
            gp = e(nc.psum_tensor("gp", [128, 512], F32))
            for pt in opp + [tp, gp]:
                warm(pt)

            for db in range(4):
                for q4 in range(4):
                    st = wo_st[:, q4 % 2, :, :]
                    t.dma("SPQ", st, wo_p[db, :, q4 * 4:(q4 + 1) * 4, :])
                    t.copy("POOL", wo_all[:, db, q4 * 4:(q4 + 1) * 4, :], st)
            bo_stage = attn_q[0:1, 1, :]
            t.dma("ACTQ", bo_stage, bo_r)
            for b in range(4):
                t.mm(gp[:, :], ones_f[0:1, :], attn_q[0:1, 1, b * 512:(b + 1) * 512])
                t.copy("DVE", bo_bc[:, b * 512:(b + 1) * 512], gp[:, :])

            def chain_a(sa):
                sl = sa % 2
                a = attn_q[:, sl, :]
                xs = x_r2[:, sl, :]
                t.dma("SPQ", xs, x_own[sa])
                ln_stats(a, sl)
                t.ts("DVE", nmr_t[:, sl:sl + 1], mv[:, sl, 0:1],
                     rstd[:, sl:sl + 1], -1.0, ALU.mult, ALU.mult)
                t.act(a, a, AF.Identity, bias=nmr_t[:, sl:sl + 1],
                      scale=rstd[:, sl:sl + 1])
                t.tt("POOL", a, a, xs, ALU.add)
                t.dma("ACTQ", x2_d[sa], a)
                if "x2" in dbg:
                    t.dma("ACTQ", dbg_aps["d_x2"][sa], a)
                ln_stats(a, sl)
                hi = hi_r[:, sl, :]
                tmp = x_r2[:, sl, :]
                t.ts("DVE", nmr_t[:, sl:sl + 1], mv[:, sl, 0:1],
                     rstd[:, sl:sl + 1], -1.0, ALU.mult, ALU.mult)
                t.act(tmp, a, AF.Identity, bias=nmr_t[:, sl:sl + 1],
                      scale=rstd[:, sl:sl + 1])
                t.act(hi, tmp, AF.Copy)
                t.tt("DVE", lo_s[:, sl, :, :],
                     tmp[:, :].rearrange("p (c x) -> p c x", x=128),
                     hi[:, :].rearrange("p (c x) -> p c x", x=128),
                     ALU.subtract)

            def chain_b(sa):
                sl = sa % 2
                hi = hi_r[:, sl, :]
                for ccg in range(4):
                    for c4 in range(4):
                        cc = ccg * 4 + c4
                        t.tr(tp[:, c4 * 256:c4 * 256 + 128],
                             hi[:, cc * 128:(cc + 1) * 128], ident[:, :],
                             inc=False)
                        t.tr(tp[:, c4 * 256 + 128:c4 * 256 + 256],
                             lo_s[:, sl, cc, :], ident[:, :], inc=(c4 == 3))
                    tpv = tp[:, :].rearrange("p (c two q) -> p c two q",
                                             c=4, two=2)
                    t.copy("DVE",
                           ln2t[:, ccg * 4:(ccg + 1) * 4,
                                sa * 128:(sa + 1) * 128],
                           tpv[:, :, 0, :])
                    t.copy("DVE", lo_s[:, sl, ccg * 4:(ccg + 1) * 4, :],
                           tpv[:, :, 1, :])
                for cc in range(NCC):
                    hiT = ln2t[:, cc, sa * 128:(sa + 1) * 128]
                    t.mm(gp[:, 0:8], hiT, gw_hi[:, cc, :],
                         start=(cc == 0), stop=False, inc=False)
                    t.mm(gp[:, 0:8], hiT, gw_lo[:, cc, :],
                         start=False, stop=False, inc=False)
                    t.mm(gp[:, 0:8], lo_s[:, sl, cc, :], gw_hi[:, cc, :],
                         start=False, stop=(cc == NCC - 1),
                         inc=(cc == NCC - 1))
                gl = g_t[:, 0, :]
                t.copy("DVE", gl, gp[:, 0:8])
                m1 = g_t[:, 1, 0:1]
                t.red("DVE", m1, gl, ALU.max)
                t1 = g_t[:, 2, :]
                t.ts("DVE", t1, gl, m1, None, ALU.subtract)
                et_ = g_t[:, 3, :]
                t.act(et_, t1, AF.Exp)
                se = g_t[:, 1, 1:2]
                t.red("DVE", se, et_, ALU.add)
                rse = g_t[:, 1, 2:3]
                t.recip(rse, se)
                pt = g_t[:, 4, :]
                t.ts("DVE", pt, et_, rse, None, ALU.mult)
                m1p = g_t[:, 1, 3:4]
                t.red("DVE", m1p, pt, ALU.max)
                nmax = g_t[:, 5, :]
                t.ts("DVE", nmax, pt, m1p, None, ALU.is_lt)
                pm = g_t[:, 6, :]
                t.tt("DVE", pm, pt, nmax, ALU.mult)
                m2p = g_t[:, 1, 4:5]
                t.red("DVE", m2p, pm, ALU.max)
                keep = g_t[:, 7, :]
                t.ts("DVE", keep, pt, m2p, None, ALU.is_ge)
                t.tt("DVE", comb[:, sa, :], pt, keep, ALU.mult)

            for sa in range(SP):
                for db in range(4):
                    p = opp[db % 2]
                    for hc in range(NCC):
                        t.mm(p[:, :], oT[:, hc, sa * 128:(sa + 1) * 128],
                             wo_all[:, db, hc, :], start=(hc == 0),
                             stop=(hc == NCC - 1), inc=(hc == NCC - 1))
                    t.tt("DVE", attn_q[:, sa % 2, db * 512:(db + 1) * 512],
                         p[:, :], bo_bc[:, db * 512:(db + 1) * 512], ALU.add)
                chain_a(sa)
                if sa > 0:
                    chain_b(sa - 1)
            chain_b(SP - 1)
            if "x2" in dbg:
                t.dma("ACTQ", dbg_aps["d_comb"], comb[:, :, :])

        o_scope.close()
        t.barrier()

        y_scope = ExitStack()
        y = y_scope.enter_context(nc.sbuf_tensor("y_sb", [128, SP, D], F32))
        with ExitStack() as ph:
            e = ph.enter_context
            h_bf = e(nc.sbuf_tensor("h_bf", [128, NIC, 1024], BF16))
            w12s = e(nc.sbuf_tensor("w12s", [128, 2, D], F32))
            w1b = e(nc.sbuf_tensor("w1b", [128, 2, NCC, 128], BF16))
            w2b = e(nc.sbuf_tensor("w2b", [128, 2, NCC, 128], BF16))
            w3st = e(nc.sbuf_tensor("w3st", [128, 4, 512], F32))
            w3b = e(nc.sbuf_tensor("w3b", [128, 2, NIC, 512], BF16))
            s1b = e(nc.sbuf_tensor("s1b", [128, 2, 512], BF16))
            h1p = [e(nc.psum_tensor(f"h1p{i}", [128, 512], F32)) for i in range(2)]
            h2p = [e(nc.psum_tensor(f"h2p{i}", [128, 512], F32)) for i in range(2)]
            eop = [e(nc.psum_tensor(f"eop{i}", [128, 512], F32)) for i in range(2)]
            for pt in h1p + h2p + eop:
                warm(pt)

            def load12(u, ic):
                rp = ic % 2
                t.dma("SPQ", w12s[:, 0, :], fw1[u, ic])
                t.copy("POOL",
                       w1b[:, rp, :, :].rearrange("p c x -> p (c x)"),
                       w12s[:, 0, :])
                t.dma("SPQ", w12s[:, 1, :], fw2[u, ic])
                t.copy("POOL",
                       w2b[:, rp, :, :].rearrange("p c x -> p (c x)"),
                       w12s[:, 1, :])

            def load3(u, cb, ic):
                t.dma("SPQ", w3st[:, ic % 4, :], fw3[u, cb, ic])
                t.copy("POOL", w3b[:, cb % 2, ic, :], w3st[:, ic % 4, :])

            load12(0, 0)
            for u in range(NU):
                for ic in range(NIC):
                    rp = ic % 2
                    if ic + 1 < NIC:
                        load12(u, ic + 1)
                    for hf in range(2):
                        for cc in range(NCC):
                            t.mm(h1p[hf][:, :], w1b[:, rp, cc, :],
                                 ln2t[:, cc, hf * 512:(hf + 1) * 512],
                                 start=(cc == 0), stop=(cc == NCC - 1),
                                 inc=(cc == NCC - 1))
                        for cc in range(NCC):
                            t.mm(h2p[hf][:, :], w2b[:, rp, cc, :],
                                 ln2t[:, cc, hf * 512:(hf + 1) * 512],
                                 start=(cc == 0), stop=(cc == NCC - 1),
                                 inc=(cc == NCC - 1))
                        t.act(s1b[:, hf, :], h1p[hf][:, :], AF.Silu)
                        t.tt("DVE", h_bf[:, ic, hf * 512:(hf + 1) * 512],
                             h2p[hf][:, :], s1b[:, hf, :], ALU.mult)
                for ic in range(NIC):
                    load3(u, 0, ic)
                for cb in range(4):
                    wp = cb % 2
                    if cb + 1 < 4:
                        for ic in range(NIC):
                            load3(u, cb + 1, ic)
                    elif u + 1 < NU:
                        load12(u + 1, 0)
                    for sa in range(SP):
                        p = eop[sa % 2]
                        for ic in range(NIC):
                            t.mm(p[:, :], h_bf[:, ic, sa * 128:(sa + 1) * 128],
                                 w3b[:, wp, ic, :], start=(ic == 0),
                                 stop=(ic == NIC - 1), inc=(ic == NIC - 1))
                        ysl = y[:, sa, cb * 512:(cb + 1) * 512]
                        if u == 0:
                            t.copy("DVE", ysl, p[:, :])
                        elif u == 1:
                            t.tt("DVE", ysl, p[:, :], ysl, ALU.add)
                        else:
                            t.stt("DVE", ysl, p[:, :], comb[:, sa, u - 2:u - 1],
                                  ysl, ALU.mult, ALU.add)

        t.barrier()

        with ExitStack() as p7:
            e7 = p7.enter_context
            x2r = e7(nc.sbuf_tensor("x2r", [128, 2, D], F32))
            outr = e7(nc.sbuf_tensor("outr", [128, 2, D], F32))
            t.dma("SPQ", x2r[:, 0, :], x2_d[0])
            t.dma("SPQ", x2r[:, 1, :], x2_d[1])
            for sa in range(SP):
                ysl = y[:, sa, :]
                ln_stats(ysl, sa % 2)
                o = outr[:, sa % 2, :]
                t.ts("DVE", o, ysl, mv[:, sa % 2, 0:1],
                     rstd[:, sa % 2:sa % 2 + 1], ALU.subtract, ALU.mult)
                t.tt("DVE", o, o, x2r[:, sa % 2, :], ALU.add)
                t.dma("ACTQ", out_d[sa], o)
                if sa + 2 < SP:
                    t.dma("SPQ", x2r[:, sa % 2, :], x2_d[sa + 2])
        y_scope.close()

        t.finish()
    return nc, dbg_aps



def _prep_shared(L0_attn_w_o, L0_attn_b_o, L0_ffn_gate_w,
                 L0_ffn_w1, L0_ffn_w2, L0_ffn_w3,
                 L0_ffn_sw1, L0_ffn_sw2, L0_ffn_sw3):
    wo_p = np.ascontiguousarray(
        L0_attn_w_o.reshape(NCC, 128, 4, 512).transpose(2, 1, 0, 3))
    bo_r = np.ascontiguousarray(L0_attn_b_o.reshape(1, D))
    gwt_p = np.ascontiguousarray(
        L0_ffn_gate_w.T.reshape(NCC, 128, 8).transpose(1, 0, 2))

    def prep_w12(w_sh, w_ex):
        sh = w_sh.T.reshape(NCC, 128, 2, NIC, 128).transpose(2, 3, 1, 0, 4)
        ex = np.stack([w_ex[ei].T.reshape(NCC, 128, NIC, 128).transpose(2, 1, 0, 3)
                       for ei in range(E)])
        return np.ascontiguousarray(
            np.concatenate([sh, ex], axis=0).reshape(NU, NIC, 128, D))

    fw1 = prep_w12(L0_ffn_sw1, L0_ffn_w1)
    fw2 = prep_w12(L0_ffn_sw2, L0_ffn_w2)

    sh3 = L0_ffn_sw3.T.reshape(2, NIC, 128, 4, 512).transpose(0, 3, 1, 2, 4)
    ex3 = np.stack([L0_ffn_w3[ei].T.reshape(NIC, 128, 4, 512).transpose(2, 0, 1, 3)
                    for ei in range(E)])
    fw3 = np.ascontiguousarray(np.concatenate([sh3, ex3], axis=0))

    masks = np.zeros((128, 4, 512), np.float32)
    for j in range(4):
        for ki in range(128):
            masks[ki, j, :128 * j + ki] = NEG
    masks = masks.astype(ml_dtypes.bfloat16)
    ident = np.eye(128, dtype=ml_dtypes.bfloat16)
    ones_b = np.ones((128, 128), ml_dtypes.bfloat16)
    ones_f = np.ones((128, 128), np.float32)
    return dict(wo_p=wo_p, bo_r=bo_r, gwt_p=gwt_p, fw1=fw1, fw2=fw2, fw3=fw3,
                masks=masks, ident=ident, ones_b=ones_b, ones_f=ones_f)


def _prep_core(c, x, w_q, b_q, w_k, b_k, w_v, b_v):
    own = np.concatenate([x[c * CH:(c + 1) * CH],
                          x[(15 - c) * CH:(16 - c) * CH]], axis=0)
    x_own = np.ascontiguousarray(own.reshape(SP, 128, D))
    cols = slice(256 * c, 256 * (c + 1))

    def wprep(w):
        return np.ascontiguousarray(
            w[:, cols].reshape(NCC, 128, 256).transpose(1, 0, 2))

    bq = b_q[cols].reshape(2, 128).T
    bk = b_k[cols].reshape(2, 128).T
    bqk = np.ascontiguousarray(
        np.stack([bq[:, 0], bq[:, 1], bk[:, 0], bk[:, 1]], axis=1))
    bv_m = np.ascontiguousarray(b_v[cols].reshape(1, 256))
    oidx = np.zeros((128, 32), np.int32)
    q1 = 15 - c
    piece = 0 if q1 < 12 else 1
    within = q1 - 8 - 4 * piece
    for cc in range(NCC):
        for p in range(128):
            oidx[p, cc * 2 + 0] = (cc * 128 + p) * 8 + c
            oidx[p, cc * 2 + 1] = piece * 8192 + (cc * 128 + p) * 4 + within
    return dict(x_own=x_own, wq_m=wprep(w_q), wk_m=wprep(w_k), wv_m=wprep(w_v),
                bqk_m=bqk, bv_m=bv_m, oidx=oidx)


_BUILD_CACHE = {}


def kernel(L0_attn_query, L0_attn_w_q, L0_attn_b_q, L0_attn_w_k, L0_attn_b_k,
           L0_attn_w_v, L0_attn_b_v, L0_attn_w_o, L0_attn_b_o, L0_ffn_gate_w,
           L0_ffn_w1, L0_ffn_w2, L0_ffn_w3, L0_ffn_sw1, L0_ffn_sw2, L0_ffn_sw3,
           _debug_outputs=(), _return_raw=False):
    x = np.asarray(L0_attn_query, np.float32).reshape(T, D)
    shared = _prep_shared(np.asarray(L0_attn_w_o, np.float32),
                          np.asarray(L0_attn_b_o, np.float32),
                          np.asarray(L0_ffn_gate_w, np.float32),
                          np.asarray(L0_ffn_w1, np.float32),
                          np.asarray(L0_ffn_w2, np.float32),
                          np.asarray(L0_ffn_w3, np.float32),
                          np.asarray(L0_ffn_sw1, np.float32),
                          np.asarray(L0_ffn_sw2, np.float32),
                          np.asarray(L0_ffn_sw3, np.float32))
    in_maps = []
    for c in range(NCORE):
        m = dict(shared)
        m.update(_prep_core(c, x,
                            np.asarray(L0_attn_w_q, np.float32),
                            np.asarray(L0_attn_b_q, np.float32),
                            np.asarray(L0_attn_w_k, np.float32),
                            np.asarray(L0_attn_b_k, np.float32),
                            np.asarray(L0_attn_w_v, np.float32),
                            np.asarray(L0_attn_b_v, np.float32)))
        in_maps.append(m)

    key = tuple(sorted(_debug_outputs))
    if key not in _BUILD_CACHE:
        _BUILD_CACHE[key] = build(_debug_outputs)
    nc, dbg_aps = _BUILD_CACHE[key]

    res = run_bass_kernel_spmd(nc, in_maps, core_ids=list(range(NCORE)))
    if _return_raw:
        return res
    full = np.empty((T, D), np.float32)
    for c in range(NCORE):
        o = res.results[c]["out"].reshape(SP * 128, D)
        full[c * CH:(c + 1) * CH] = o[:CH]
        full[(15 - c) * CH:(16 - c) * CH] = o[CH:]
    return full.reshape(1, T, D)


# version 15
# speedup vs baseline: 1.1483x; 1.0058x over previous
import numpy as np
import ml_dtypes
from contextlib import ExitStack

import concourse.bass as bass
import concourse.mybir as mybir
from concourse.bass_utils import run_bass_kernel_spmd

F32 = mybir.dt.float32
BF16 = mybir.dt.bfloat16
I32 = mybir.dt.int32
ALU = mybir.AluOpType
AF = mybir.ActivationFunctionType
AX = mybir.AxisListType

T = 8192
D = 2048
NH = 16
HS = 128
NCORE = 8
CH = 512
NCHK = 16
SP = 8
E = 8
IM = 1408
NIC = 11
ISH = 2816
NU = 10
NCC = 16
EPS = 1e-5
ISR = 0.08838834764831845
NEG = -1.0e9


def tb_of_chunk(c):
    return 2 * c if c < 8 else 2 * (15 - c) + 1



def _rect(ap):
    t = ap.tensor
    name = t.name
    try:
        off = int(ap.offset)
    except Exception:
        return (name, 0, 1 << 20, 0, 1 << 60)
    dims = list(ap.ap)
    if not dims:
        return (name, 0, 1 << 20, off, off + 1)
    space = str(ap.space)
    asz = mybir.dt.size(ap.dtype)
    if "DRAM" in space:
        lo = off
        hi = off
        for st, cnt in dims:
            if st >= 0:
                hi += st * (cnt - 1)
            else:
                lo += st * (cnt - 1)
        return (name, 0, 1 << 20, lo * asz, (hi + 1) * asz)
    row = 1
    for s in t.shape[1:]:
        row *= s
    tsz = mybir.dt.size(t.dtype)
    if tsz != asz:
        row = row * tsz // asz
    p0 = off // row
    f_lo = off % row
    f_hi = f_lo
    nump = 1
    for st, cnt in dims:
        if st != 0 and st % row == 0:
            nump = max(nump, (cnt - 1) * (st // row) + 1)
        else:
            if st >= 0:
                f_hi += st * (cnt - 1)
            else:
                f_lo += st * (cnt - 1)
    if "PSUM" in space:
        return (name, 0, 128, 0, 1 << 30)
    return (name, p0, p0 + nump, f_lo * asz, (f_hi + 1) * asz)


def _overlap(a, b):
    return (a[0] == b[0]) and not (
        a[2] <= b[1] or b[2] <= a[1] or a[4] <= b[3] or b[4] <= a[3]
    )


def _contains(a, b):
    return (a[0] == b[0]) and a[1] <= b[1] and a[2] >= b[2] and a[3] <= b[3] and a[4] >= b[4]


class Trk:
    def __init__(self, nc):
        self.nc = nc
        self.streams = {}
        self.cnt = {}
        self.waited = {}
        self.writes = {}
        self.reads = {}
        for name, eng, ekey, amt in [
            ("TE", nc.tensor, "te", 1),
            ("DVE", nc.vector, "dve", 1),
            ("ACT", nc.scalar, "act", 1),
            ("POOL", nc.gpsimd, "pool", 1),
            ("SPQ", nc.sync, "sp", 16),
            ("ACTQ", nc.scalar, "act", 16),
            ("POOLQ", nc.gpsimd, "pool", 16),
        ]:
            sem = nc.alloc_semaphore(f"trk_{name}")
            self.streams[name] = (eng, ekey, sem, amt)
            self.cnt[name] = 0

    def _deps_for(self, rect, want_writes, want_reads):
        deps = []
        if want_writes:
            for r, s, v in self.writes.get(rect[0], ()):
                if _overlap(rect, r):
                    deps.append((s, v))
        if want_reads:
            for r, s, v in self.reads.get(rect[0], ()):
                if _overlap(rect, r):
                    deps.append((s, v))
        return deps

    def _emit_waits(self, stream, deps):
        eng, ekey, ssem, _ = self.streams[stream]
        best = {}
        selfmax = 0
        for s, v in deps:
            if s == stream:
                if stream not in ("TE", "SPQ", "ACTQ", "POOLQ"):
                    selfmax = max(selfmax, v)
                continue
            _, dekey, dsem, _ = self.streams[s]
            if s == "TE" and ekey == "te":
                continue
            if dekey == ekey and dsem.name == ssem.name:
                continue
            key = (ekey, dsem.name)
            if self.waited.get(key, 0) >= v:
                continue
            cur = best.get(dsem.name)
            if cur is None or v > cur[1]:
                best[dsem.name] = (dsem, v, key)
        for dsem, v, key in best.values():
            eng.wait_ge(dsem, v)
            self.waited[key] = v
        if selfmax > 0:
            key = (ekey + "#self", ssem.name)
            if self.waited.get(key, 0) < selfmax:
                eng.wait_ge(ssem, selfmax)
                self.waited[key] = selfmax

    def _record(self, rect, stream, value, is_write):
        name = rect[0]
        if is_write:
            wl = self.writes.setdefault(name, [])
            wl[:] = [e for e in wl if not _contains(rect, e[0])]
            wl.append((rect, stream, value))
            rl = self.reads.setdefault(name, [])
            rl[:] = [e for e in rl if not _contains(rect, e[0])]
        else:
            rl = self.reads.setdefault(name, [])
            rl[:] = [e for e in rl
                     if not (_contains(rect, e[0]) and e[1] == stream)]
            rl.append((rect, stream, value))

    def op(self, stream, fn, reads=(), writes=(), inc=True):
        eng, ekey, sem, amt = self.streams[stream]
        rrects = [_rect(a) for a in reads]
        wrects = [_rect(a) for a in writes]
        deps = []
        for r in rrects:
            deps += self._deps_for(r, True, False)
        for r in wrects:
            deps += self._deps_for(r, True, True)
        self._emit_waits(stream, deps)
        inst = fn(eng)
        if inc:
            inst.then_inc(sem, amt)
            self.cnt[stream] += amt
        val = self.cnt[stream] + (0 if inc else amt)
        for r in rrects:
            self._record(r, stream, val, False)
        for r in wrects:
            self._record(r, stream, val, True)
        return inst

    def dma(self, queue, out, in_, **kw):
        return self.op(queue, lambda e: e.dma_start(out=out, in_=in_, **kw),
                       reads=[in_], writes=[out])

    def mm(self, out, lhsT, rhs, start=True, stop=True, inc=True, **kw):
        return self.op("TE",
                       lambda e: e.matmul(out, lhsT, rhs, start=start, stop=stop, **kw),
                       reads=[lhsT, rhs], writes=[out], inc=inc)

    def tr(self, out, in_, ident, inc=True):
        return self.op("TE", lambda e: e.transpose(out, in_, ident),
                       reads=[in_, ident], writes=[out], inc=inc)

    def act(self, out, in_, func, bias=0.0, scale=1.0, accum_out=None):
        reads = [in_]
        writes = [out]
        if hasattr(bias, "tensor"):
            reads.append(bias)
        if accum_out is not None:
            writes = [out, accum_out]
            return self.op("ACT",
                           lambda e: e.activation(out, in_, func, bias=bias,
                                                  scale=scale, accum_out=accum_out),
                           reads=reads, writes=writes)
        return self.op("ACT",
                       lambda e: e.activation(out, in_, func, bias=bias, scale=scale),
                       reads=reads, writes=writes)

    def copy(self, stream, out, in_):
        return self.op(stream, lambda e: e.tensor_copy(out, in_),
                       reads=[in_], writes=[out])

    def tt(self, stream, out, in0, in1, alu):
        return self.op(stream, lambda e: e.tensor_tensor(out, in0, in1, alu),
                       reads=[in0, in1], writes=[out])

    def ts(self, stream, out, in0, s1, s2, op0, op1=None):
        reads = [in0]
        if hasattr(s1, "tensor"):
            reads.append(s1)
        if s2 is not None and hasattr(s2, "tensor"):
            reads.append(s2)
        if op1 is None:
            return self.op(stream, lambda e: e.tensor_scalar(out, in0, s1, s2, op0),
                           reads=reads, writes=[out])
        return self.op(stream, lambda e: e.tensor_scalar(out, in0, s1, s2, op0, op1),
                       reads=reads, writes=[out])

    def stt(self, stream, out, in0, scalar, in1, op0, op1):
        reads = [in0, in1]
        if hasattr(scalar, "tensor"):
            reads.append(scalar)
        return self.op(stream,
                       lambda e: e.scalar_tensor_tensor(out, in0, scalar, in1, op0, op1),
                       reads=reads, writes=[out])

    def red(self, stream, out, in_, alu):
        return self.op(stream, lambda e: e.tensor_reduce(out, in_, AX.X, alu),
                       reads=[in_], writes=[out])

    def recip(self, out, in_):
        return self.op("DVE", lambda e: e.reciprocal(out, in_),
                       reads=[in_], writes=[out])

    def bn(self, out, in_, nchunks):
        n = in_.shape[-1] // nchunks
        for i in range(nchunks):
            sl = in_[:, i * n:(i + 1) * n]
            self.op("DVE", lambda e, i=i, sl=sl: e.bn_stats(
                self._bn_buf[:, i, :], sl),
                reads=[sl], writes=[self._bn_buf[:, i, :]])
        self.op("DVE", lambda e: e.bn_aggr(out, self._bn_buf[:, 0:nchunks, :]),
                reads=[self._bn_buf[:, 0:nchunks, :]], writes=[out])

    def collective(self, kind, aluop, ins, outs, replica_groups):
        sem = self.nc.alloc_semaphore(f"cc_{len(self.streams)}")
        eng, ekey, _, _ = self.streams["POOL"]
        deps = []
        for a in ins:
            deps += self._deps_for(_rect(a), True, False)
        for a in outs:
            deps += self._deps_for(_rect(a), True, True)
        self._emit_waits("POOL", deps)
        inst = eng.collective_compute(kind, aluop, replica_groups=replica_groups,
                                      ins=ins, outs=outs)
        inst.then_inc(sem)
        sname = f"CC{len(self.streams)}"
        self.streams[sname] = (eng, "pool", sem, 1)
        self.cnt[sname] = 1
        for a in outs:
            self._record(_rect(a), sname, 1, True)
        for a in ins:
            self._record(_rect(a), sname, 1, False)
        return inst

    def indirect(self, out, in_, idx):
        from concourse.bass import IndirectOffsetOnAxis
        return self.op("POOLQ",
                       lambda e: e.indirect_dma_start(
                           out, None, in_, IndirectOffsetOnAxis(idx, 0)),
                       reads=[in_, idx], writes=[out])

    def barrier(self):
        snap = [(sname, tup[2], self.cnt.get(sname, 0))
                for sname, tup in self.streams.items()
                if self.cnt.get(sname, 0) > 0 and not sname.startswith("CC")]
        for wname in ("TE", "DVE", "ACT", "POOL", "SPQ"):
            weng, wekey, _, _ = self.streams[wname]
            for sname, sem, v in snap:
                key = (wekey, sem.name)
                if self.waited.get(key, 0) < v:
                    weng.wait_ge(sem, v)
                    self.waited[key] = v

    def finish(self):
        eng, ekey, _, _ = self.streams["POOL"]
        for sname, (seng, sekey, sem, amt) in list(self.streams.items()):
            v = self.cnt.get(sname, 0)
            if v > 0:
                key = (ekey, sem.name)
                if self.waited.get(key, 0) < v:
                    eng.wait_ge(sem, v)
                    self.waited[key] = v



def build(debug_outputs=()):
    nc = bass.Bass("TRN2", target_bir_lowering=False)
    dbg = set(debug_outputs)

    x_own = nc.dram_tensor("x_own", [SP, 128, D], F32, kind="ExternalInput").ap()
    wq_m = nc.dram_tensor("wq_m", [128, NCC, 256], F32, kind="ExternalInput").ap()
    wk_m = nc.dram_tensor("wk_m", [128, NCC, 256], F32, kind="ExternalInput").ap()
    wv_m = nc.dram_tensor("wv_m", [128, NCC, 256], F32, kind="ExternalInput").ap()
    bqk_m = nc.dram_tensor("bqk_m", [128, 4], F32, kind="ExternalInput").ap()
    bv_m = nc.dram_tensor("bv_m", [1, 256], F32, kind="ExternalInput").ap()
    bo_r = nc.dram_tensor("bo_r", [1, D], F32, kind="ExternalInput").ap()
    wo_p = nc.dram_tensor("wo_p", [4, 128, NCC, 512], F32, kind="ExternalInput").ap()
    gwt_p = nc.dram_tensor("gwt_p", [128, NCC, 8], F32, kind="ExternalInput").ap()
    fw1 = nc.dram_tensor("fw1", [NU, NIC, 128, D], F32, kind="ExternalInput").ap()
    fw2 = nc.dram_tensor("fw2", [NU, NIC, 128, D], F32, kind="ExternalInput").ap()
    fw3 = nc.dram_tensor("fw3", [NU, 4, NIC, 128, 512], F32, kind="ExternalInput").ap()
    masks_d = nc.dram_tensor("masks", [128, 4, 512], BF16, kind="ExternalInput").ap()
    ident_d = nc.dram_tensor("ident", [128, 128], BF16, kind="ExternalInput").ap()
    ones_b_d = nc.dram_tensor("ones_b", [128, 128], BF16, kind="ExternalInput").ap()
    ones_f_d = nc.dram_tensor("ones_f", [128, 128], F32, kind="ExternalInput").ap()
    oidx_d = nc.dram_tensor("oidx", [128, 32], I32, kind="ExternalInput").ap()

    out_d = nc.dram_tensor("out", [SP, 128, D], F32, kind="ExternalOutput").ap()

    ln1_own = nc.dram_tensor("ln1_own", [2, D, 512], BF16, kind="Internal").ap()
    ln1_alla = nc.dram_tensor("ln1_alla", [NCORE * D, 512], BF16, kind="Internal", addr_space="Shared").ap()
    ln1_allb = nc.dram_tensor("ln1_allb", [NCORE * D, 512], BF16, kind="Internal", addr_space="Shared").ap()
    ccwarm_i = nc.dram_tensor("ccwarm_i", [16, 16], BF16, kind="Internal").ap()
    ccwarm_o = nc.dram_tensor("ccwarm_o", [128, 16], BF16, kind="Internal", addr_space="Shared").ap()
    x2_d = nc.dram_tensor("x2_d", [SP, 128, D], F32, kind="Internal").ap()

    o_own_a = nc.dram_tensor("o_own_a", [256, T // 2], BF16, kind="Internal").ap()
    o_own_b = nc.dram_tensor("o_own_b", [256, T // 4], BF16, kind="Internal").ap()
    o_own_c = nc.dram_tensor("o_own_c", [256, T // 4], BF16, kind="Internal").ap()
    o_all_a = nc.dram_tensor("o_all_a", [D, T // 2], BF16, kind="Internal", addr_space="Shared").ap()
    o_all_bc = nc.dram_tensor("o_all_bc", [2, D, T // 4], BF16, kind="Internal", addr_space="Shared").ap()

    dbg_aps = {}
    if "ln1_all" in dbg:
        dbg_aps["d_ln1"] = nc.dram_tensor("d_ln1", [NCORE * D, 1024], BF16,
                                          kind="ExternalOutput").ap()
    if "kqv" in dbg:
        dbg_aps["d_k"] = nc.dram_tensor("d_k", [128, 2, NCHK, 512], BF16,
                                        kind="ExternalOutput").ap()
        dbg_aps["d_q"] = nc.dram_tensor("d_q", [128, 2, NCHK, 512], BF16,
                                        kind="ExternalOutput").ap()
        dbg_aps["d_v"] = nc.dram_tensor("d_v", [128, 64, 256], BF16,
                                        kind="ExternalOutput").ap()
    if "o_all" in dbg:
        dbg_aps["d_o"] = nc.dram_tensor("d_o", [D, T], BF16,
                                        kind="ExternalOutput").ap()
    if "x2" in dbg:
        dbg_aps["d_x2"] = nc.dram_tensor("d_x2", [SP, 128, D], F32,
                                         kind="ExternalOutput").ap()
        dbg_aps["d_comb"] = nc.dram_tensor("d_comb", [128, SP, 8], F32,
                                           kind="ExternalOutput").ap()

    t = Trk(nc)
    RG = [list(range(NCORE))]

    with ExitStack() as top:
        et = top.enter_context
        ident = et(nc.sbuf_tensor("ident_sb", [128, 128], BF16))
        ones_b = et(nc.sbuf_tensor("ones_b_sb", [128, 128], BF16))
        ones_f = et(nc.sbuf_tensor("ones_f_sb", [128, 128], F32))
        bqk = et(nc.sbuf_tensor("bqk_sb", [128, 4], F32))
        bv_row = et(nc.sbuf_tensor("bv_row", [1, 256], F32))
        bv_bc = et(nc.sbuf_tensor("bv_bc", [128, 256], F32))
        gwt = et(nc.sbuf_tensor("gwt_sb", [128, NCC, 8], F32))
        gw_hi = et(nc.sbuf_tensor("gw_hi", [128, NCC, 8], BF16))
        gw_lo = et(nc.sbuf_tensor("gw_lo", [128, NCC, 8], BF16))
        shavg = et(nc.sbuf_tensor("shavg", [128, 1], F32))
        sh_tmp = et(nc.sbuf_tensor("sh_tmp", [128, 2], F32))
        sh_row = et(nc.sbuf_tensor("sh_row", [1, 2], F32))
        oidx = et(nc.sbuf_tensor("oidx_sb", [128, 32], I32))
        comb = et(nc.sbuf_tensor("comb_sb", [128, SP, 8], F32))
        bnb = et(nc.sbuf_tensor("bn_buf", [128, 4, 6], F32))
        mv = et(nc.sbuf_tensor("mv_sb", [128, 2, 2], F32))
        rstd = et(nc.sbuf_tensor("rstd_sb", [128, 2], F32))
        eps_t = et(nc.sbuf_tensor("eps_t", [128, 1], F32))
        t._bn_buf = bnb
        t.op("DVE", lambda e: e.memset(eps_t[:, :], EPS), writes=[eps_t[:, :]])

        o_scope = ExitStack()
        oT = o_scope.enter_context(nc.sbuf_tensor("oT_sb", [128, NCC, 1024], BF16, side="right"))
        w_scope = ExitStack()
        ew = w_scope.enter_context
        wst = ew(nc.sbuf_tensor("w_stage", [128, 8, 256], F32))
        wqb = ew(nc.sbuf_tensor("wq_b", [128, NCC, 256], BF16))
        wkb = ew(nc.sbuf_tensor("wk_b", [128, NCC, 256], BF16))
        wvb = ew(nc.sbuf_tensor("wv_b", [128, NCC, 256], BF16))

        t.dma("ACTQ", ident[:, :], ident_d)
        t.dma("ACTQ", ones_b[:, :], ones_b_d)
        t.dma("ACTQ", ones_f[:, :], ones_f_d)
        t.dma("ACTQ", bqk[:, :], bqk_m)
        t.dma("ACTQ", bv_row[:, :], bv_m)
        t.dma("ACTQ", gwt[:, :, :], gwt_p)
        t.dma("ACTQ", oidx[:, :], oidx_d)
        t.collective("AllGather", ALU.bypass, [ccwarm_i], [ccwarm_o], RG)


        def warm(pt):
            ap = pt[:, :]
            if ap.dtype == BF16:
                ap = ap.bitcast(F32)
            t.op("DVE", lambda e, ap=ap: e.memset(ap, 0.0), writes=[ap])

        def ln_stats(src, slot):
            t.bn(mv[:, slot, :], src, 4)
            t.act(rstd[:, slot:slot + 1], mv[:, slot, 1:2], AF.Ln, bias=eps_t[:, :])
            t.act(rstd[:, slot:slot + 1], rstd[:, slot:slot + 1], AF.Exp, scale=-0.5)

        with ExitStack() as ph:
            e = ph.enter_context
            x_ring = e(nc.sbuf_tensor("x_ring", [128, 2, D], F32))
            ln1_tok = e(nc.sbuf_tensor("ln1_tok", [128, SP, D], BF16))
            tr_st = e(nc.sbuf_tensor("tr_st", [128, 2, 1024], BF16))
            cps = e(nc.psum_tensor("cps", [128, 512], F32))
            trp = [e(nc.psum_tensor(f"trp{i}", [128, 1024], BF16)) for i in range(2)]
            for pt in [cps] + trp:
                warm(pt)

            t.mm(cps[:, 0:256], ones_f[0:1, :], bv_row[0:1, :])
            t.copy("DVE", bv_bc[:, :], cps[:, 0:256])
            t.stt("DVE", sh_tmp[:, :], bqk[:, 0:2], -ISR, bqk[:, 2:4],
                  ALU.mult, ALU.mult)
            t.mm(cps[0:1, 0:2], ones_f[:, 0:1], sh_tmp[:, :])
            t.copy("DVE", sh_row[0:1, :], cps[0:1, 0:2])
            t.red("DVE", sh_row[0:1, 0:1], sh_row[0:1, :], ALU.add)
            t.ts("DVE", sh_row[0:1, 0:1], sh_row[0:1, 0:1], 0.5, None, ALU.mult)
            t.mm(cps[:, 0:1], ones_f[0:1, :], sh_row[0:1, 0:1])
            t.copy("DVE", shavg[:, :], cps[:, 0:1])
            t.copy("DVE", gw_hi[:, :, :], gwt[:, :, :])
            t.tt("DVE", gw_lo[:, :, :], gwt[:, :, :], gw_hi[:, :, :], ALU.subtract)

            for s in range(SP):
                xs = x_ring[:, s % 2, :]
                t.dma("SPQ", xs, x_own[s])
                ln_stats(xs, s % 2)
                t.ts("DVE", ln1_tok[:, s, :], xs,
                     mv[:, s % 2, 0:1], rstd[:, s % 2:s % 2 + 1],
                     ALU.subtract, ALU.mult)
            for h in range(2):
                for cc in range(NCC):
                    pb = trp[cc % 2]
                    for si in range(4):
                        sa = h * 4 + si
                        t.tr(pb[:, si * 128:(si + 1) * 128],
                             ln1_tok[:, sa, cc * 128:(cc + 1) * 128], ident[:, :],
                             inc=(si == 3))
                    st = tr_st[:, cc % 2, 0:512]
                    t.copy("DVE", st, pb[:, 0:512])
                    t.dma("ACTQ", ln1_own[h, cc * 128:(cc + 1) * 128, :], st)
                t.collective("AllGather", ALU.bypass, [ln1_own[h]],
                             [ln1_alla if h == 0 else ln1_allb], RG)
            for wsrc, wdst in ((wq_m, wqb), (wk_m, wkb), (wv_m, wvb)):
                for hh in range(2):
                    t.dma("SPQ", wst[:, :, :], wsrc[:, hh * 8:(hh + 1) * 8, :])
                    t.copy("DVE", wdst[:, hh * 8:(hh + 1) * 8, :], wst[:, :, :])

        t.barrier()

        kqv_scope = ExitStack()
        e = kqv_scope.enter_context
        masks = e(nc.sbuf_tensor("masks_sb", [128, 4, 512], BF16))
        k_res = e(nc.sbuf_tensor("k_res", [128, 2, NCHK, 512], BF16))
        q_res = e(nc.sbuf_tensor("q_res", [128, 2, NCHK, 512], BF16))
        v_res = e(nc.sbuf_tensor("v_res", [128, 64, 256], BF16))

        t.dma("ACTQ", masks[:, :, :], masks_d)
        with ExitStack() as ph:
            e = ph.enter_context
            lnr = e(nc.sbuf_tensor("lnall_ring", [128, 2, NCC, 512], BF16))
            kp = [e(nc.psum_tensor(f"kp{i}", [128, 512], F32)) for i in range(2)]
            qp = [e(nc.psum_tensor(f"qp{i}", [128, 512], F32)) for i in range(2)]
            vp = [e(nc.psum_tensor(f"vp{i}", [128, 512], F32)) for i in range(2)]
            for pt in kp + qp + vp:
                warm(pt)

            def lnall_block(tb):
                src = ln1_alla if tb % 2 == 0 else ln1_allb
                r = tb // 2
                return src.rearrange("(r cc p) q -> p r cc q", p=128, cc=NCC)[:, r]

            tb_order = list(range(0, NCHK, 2)) + list(range(1, NCHK, 2))
            t.dma("SPQ", lnr[:, 0, :, :], lnall_block(tb_order[0]))
            for ti in range(NCHK):
                tb = tb_order[ti]
                if ti + 1 < NCHK:
                    t.dma("SPQ", lnr[:, (ti + 1) % 2, :, :],
                          lnall_block(tb_order[ti + 1]))
                ln = lnr[:, ti % 2, :, :]
                for hp in range(2):
                    for cc in range(NCC):
                        t.mm(kp[hp][:, :], wkb[:, cc, hp * 128:(hp + 1) * 128],
                             ln[:, cc, :], start=(cc == 0), stop=(cc == NCC - 1),
                             inc=(cc == NCC - 1))
                    t.act(k_res[:, hp, tb, :], kp[hp][:, :], AF.Identity,
                          bias=bqk[:, 2 + hp:3 + hp])
                    for cc in range(NCC):
                        t.mm(qp[hp][:, :], wqb[:, cc, hp * 128:(hp + 1) * 128],
                             ln[:, cc, :], start=(cc == 0), stop=(cc == NCC - 1),
                             inc=(cc == NCC - 1))
                    t.act(q_res[:, hp, tb, :], qp[hp][:, :], AF.Identity,
                          bias=bqk[:, hp:hp + 1])
                for j in range(4):
                    p = vp[j % 2]
                    for cc in range(NCC):
                        t.mm(p[:, 0:256], ln[:, cc, j * 128:(j + 1) * 128],
                             wvb[:, cc, :], start=(cc == 0), stop=(cc == NCC - 1),
                             inc=(cc == NCC - 1))
                    t.tt("DVE", v_res[:, tb * 4 + j, :], p[:, 0:256], bv_bc[:, :],
                         ALU.add)
            if "kqv" in dbg:
                t.dma("ACTQ", dbg_aps["d_k"], k_res[:, :, :, :])
                t.dma("ACTQ", dbg_aps["d_q"], q_res[:, :, :, :])
                t.dma("ACTQ", dbg_aps["d_v"], v_res[:, :, :])

        t.barrier()

        with ExitStack() as ph:
            e = ph.enter_context
            p_ring = e(nc.sbuf_tensor("p_ring", [128, 4, 1024], BF16))
            o_st = e(nc.sbuf_tensor("o_st", [128, 2, 2, 512], BF16))
            rcp = e(nc.sbuf_tensor("rcp_sb", [1, 2, 512], F32))
            r_sb = e(nc.sbuf_tensor("r_sb", [128, 2, 512], F32))
            lg = [e(nc.psum_tensor(f"lg{i}", [128, 1024], F32)) for i in range(2)]
            ops = [e(nc.psum_tensor(f"op{i}", [128, 512], F32)) for i in range(2)]
            sps = [e(nc.psum_tensor(f"sp{i}", [128, 512], F32)) for i in range(2)]
            for pt in lg + ops + sps:
                warm(pt)

            o_flat_a = o_all_a.rearrange("r (k c) -> (r k) c", c=512)
            o_flat_bc = o_all_bc.rearrange("h r (k c) -> (h r k) c", c=512)

            def norm_store(qb):
                for hp in range(2):
                    t.recip(rcp[0:1, hp, :], sps[hp][0:1, :])
                    t.mm(sps[hp][:, :], ones_f[0:1, :], rcp[0:1, hp, :])
                    t.copy("DVE", r_sb[:, hp, :], sps[hp][:, :])
                    t.tt("DVE", o_st[:, qb % 2, hp, :], ops[hp][:, :],
                         r_sb[:, hp, :], ALU.mult)
                if qb < 8:
                    half, cq = o_own_a, qb
                elif qb < 12:
                    half, cq = o_own_b, qb - 8
                else:
                    half, cq = o_own_c, qb - 12
                t.dma("ACTQ",
                      half.rearrange("(h p) q -> p h q", p=128)[
                          :, :, cq * 512:(cq + 1) * 512],
                      o_st[:, qb % 2, :, :])

            def sums_av(n, qb_n, tc, j, first, last):
                for hp in range(2):
                    phalf = p_ring[:, n % 4, hp * 512:(hp + 1) * 512]
                    t.mm(sps[hp][0:64, :], ones_b[:, 0:64], phalf,
                         start=first, stop=last, inc=False)
                    t.mm(ops[hp][:, :],
                         v_res[:, tc * 4 + j, hp * 128:(hp + 1) * 128],
                         phalf, start=first, stop=last, inc=(hp == 1))

            tiles = []
            for qb in range(NCHK):
                for c in range(qb + 1):
                    for j in range(4):
                        tiles.append((qb, c, j, c == 0 and j == 0,
                                      c == qb and j == 3))
            def fire_norm(qb):
                norm_store(qb)
                if qb == 7:
                    t.collective("AllGather", ALU.bypass,
                                 [o_own_a], [o_all_a], RG)
                    for cc in range(NCC):
                        t.indirect(oT[:, cc, 0:512], o_flat_a,
                                   oidx[:, cc * 2:cc * 2 + 1])
                elif qb == 11:
                    t.collective("AllGather", ALU.bypass,
                                 [o_own_b], [o_all_bc[0]], RG)

            pq = []
            pend_norm = None
            gn = 0
            for (qb, c, j, first, last) in tiles:
                g = lg[gn % 2]
                ps = p_ring[:, gn % 4, :]
                tc = tb_of_chunk(c)
                tq = tb_of_chunk(qb)
                t.mm(g[:, 0:512], k_res[:, 0, tc, j * 128:(j + 1) * 128],
                     q_res[:, 0, tq, :], inc=False)
                t.mm(g[:, 512:1024], k_res[:, 1, tc, j * 128:(j + 1) * 128],
                     q_res[:, 1, tq, :], inc=True)
                if c == qb:
                    gv = g[:, :].rearrange("p (h q) -> p h q", h=2)
                    t.tt("DVE", gv, gv,
                         masks[:, j:j + 1, :].broadcast_to([128, 2, 512]),
                         ALU.add)
                t.act(ps, g[:, :], AF.Exp, bias=shavg[:, :], scale=ISR)
                if pend_norm is not None:
                    fire_norm(pend_norm)
                    pend_norm = None
                if len(pq) >= 2:
                    p = pq.pop(0)
                    sums_av(p[5], p[0], tb_of_chunk(p[1]), p[2], p[3], p[4])
                    if p[4]:
                        pend_norm = p[0]
                pq.append((qb, c, j, first, last, gn))
                gn += 1
            for p in pq:
                if pend_norm is not None:
                    fire_norm(pend_norm)
                    pend_norm = None
                sums_av(p[5], p[0], tb_of_chunk(p[1]), p[2], p[3], p[4])
                if p[4]:
                    pend_norm = p[0]
            norm_store(pend_norm)
            t.collective("AllGather", ALU.bypass, [o_own_c], [o_all_bc[1]], RG)
            for cc in range(NCC):
                t.indirect(oT[:, cc, 512:1024], o_flat_bc,
                           oidx[:, cc * 2 + 1:cc * 2 + 2])
            if "o_all" in dbg:
                t.dma("ACTQ", dbg_aps["d_o"][:, 0:T // 2], o_all_a)
                t.dma("ACTQ", dbg_aps["d_o"][:, T // 2:3 * T // 4], o_all_bc[0])
                t.dma("ACTQ", dbg_aps["d_o"][:, 3 * T // 4:T], o_all_bc[1])

        kqv_scope.close()
        w_scope.close()
        t.barrier()

        ln2t = top.enter_context(nc.sbuf_tensor("ln2t", [128, NCC, 1024], BF16))
        with ExitStack() as ph:
            e = ph.enter_context
            attn_q = e(nc.sbuf_tensor("attn_q", [128, 2, D], F32))
            wo_st = e(nc.sbuf_tensor("wo_st", [128, 2, 4, 512], F32))
            wo_all = e(nc.sbuf_tensor("wo_all", [128, 4, NCC, 512], BF16))
            bo_bc = e(nc.sbuf_tensor("bo_bc", [128, D], F32))
            x_r2 = e(nc.sbuf_tensor("x_r2", [128, 2, D], F32))
            hi_r = e(nc.sbuf_tensor("hi_r", [128, 2, D], BF16))
            lo_s = e(nc.sbuf_tensor("lo_s", [128, 2, NCC, 128], BF16))
            g_t = e(nc.sbuf_tensor("g_t", [128, 8, 8], F32))
            nmr_t = e(nc.sbuf_tensor("nmr_t", [128, 2], F32))
            opp = [e(nc.psum_tensor(f"opp{i}", [128, 512], F32)) for i in range(2)]
            tp = e(nc.psum_tensor("tp", [128, 1024], BF16))
            gp = e(nc.psum_tensor("gp", [128, 512], F32))
            for pt in opp + [tp, gp]:
                warm(pt)

            for db in range(4):
                for q4 in range(4):
                    st = wo_st[:, q4 % 2, :, :]
                    t.dma("SPQ", st, wo_p[db, :, q4 * 4:(q4 + 1) * 4, :])
                    t.copy("POOL", wo_all[:, db, q4 * 4:(q4 + 1) * 4, :], st)
            bo_stage = attn_q[0:1, 1, :]
            t.dma("ACTQ", bo_stage, bo_r)
            for b in range(4):
                t.mm(gp[:, :], ones_f[0:1, :], attn_q[0:1, 1, b * 512:(b + 1) * 512])
                t.copy("DVE", bo_bc[:, b * 512:(b + 1) * 512], gp[:, :])

            def chain_a(sa):
                sl = sa % 2
                a = attn_q[:, sl, :]
                xs = x_r2[:, sl, :]
                t.dma("SPQ", xs, x_own[sa])
                ln_stats(a, sl)
                t.ts("DVE", nmr_t[:, sl:sl + 1], mv[:, sl, 0:1],
                     rstd[:, sl:sl + 1], -1.0, ALU.mult, ALU.mult)
                t.act(a, a, AF.Identity, bias=nmr_t[:, sl:sl + 1],
                      scale=rstd[:, sl:sl + 1])
                t.tt("POOL", a, a, xs, ALU.add)
                t.dma("ACTQ", x2_d[sa], a)
                if "x2" in dbg:
                    t.dma("ACTQ", dbg_aps["d_x2"][sa], a)
                ln_stats(a, sl)
                hi = hi_r[:, sl, :]
                tmp = x_r2[:, sl, :]
                t.ts("DVE", nmr_t[:, sl:sl + 1], mv[:, sl, 0:1],
                     rstd[:, sl:sl + 1], -1.0, ALU.mult, ALU.mult)
                t.act(tmp, a, AF.Identity, bias=nmr_t[:, sl:sl + 1],
                      scale=rstd[:, sl:sl + 1])
                t.act(hi, tmp, AF.Copy)
                t.tt("DVE", lo_s[:, sl, :, :],
                     tmp[:, :].rearrange("p (c x) -> p c x", x=128),
                     hi[:, :].rearrange("p (c x) -> p c x", x=128),
                     ALU.subtract)

            def chain_b(sa):
                sl = sa % 2
                hi = hi_r[:, sl, :]
                for ccg in range(4):
                    for c4 in range(4):
                        cc = ccg * 4 + c4
                        t.tr(tp[:, c4 * 256:c4 * 256 + 128],
                             hi[:, cc * 128:(cc + 1) * 128], ident[:, :],
                             inc=False)
                        t.tr(tp[:, c4 * 256 + 128:c4 * 256 + 256],
                             lo_s[:, sl, cc, :], ident[:, :], inc=(c4 == 3))
                    tpv = tp[:, :].rearrange("p (c two q) -> p c two q",
                                             c=4, two=2)
                    t.copy("DVE",
                           ln2t[:, ccg * 4:(ccg + 1) * 4,
                                sa * 128:(sa + 1) * 128],
                           tpv[:, :, 0, :])
                    t.copy("DVE", lo_s[:, sl, ccg * 4:(ccg + 1) * 4, :],
                           tpv[:, :, 1, :])
                for cc in range(NCC):
                    hiT = ln2t[:, cc, sa * 128:(sa + 1) * 128]
                    t.mm(gp[:, 0:8], hiT, gw_hi[:, cc, :],
                         start=(cc == 0), stop=False, inc=False)
                    t.mm(gp[:, 0:8], hiT, gw_lo[:, cc, :],
                         start=False, stop=False, inc=False)
                    t.mm(gp[:, 0:8], lo_s[:, sl, cc, :], gw_hi[:, cc, :],
                         start=False, stop=(cc == NCC - 1),
                         inc=(cc == NCC - 1))
                gl = g_t[:, 0, :]
                t.copy("DVE", gl, gp[:, 0:8])
                m1 = g_t[:, 1, 0:1]
                t.red("DVE", m1, gl, ALU.max)
                t1 = g_t[:, 2, :]
                t.ts("DVE", t1, gl, m1, None, ALU.subtract)
                et_ = g_t[:, 3, :]
                t.act(et_, t1, AF.Exp)
                se = g_t[:, 1, 1:2]
                t.red("DVE", se, et_, ALU.add)
                rse = g_t[:, 1, 2:3]
                t.recip(rse, se)
                pt = g_t[:, 4, :]
                t.ts("DVE", pt, et_, rse, None, ALU.mult)
                m1p = g_t[:, 1, 3:4]
                t.red("DVE", m1p, pt, ALU.max)
                nmax = g_t[:, 5, :]
                t.ts("DVE", nmax, pt, m1p, None, ALU.is_lt)
                pm = g_t[:, 6, :]
                t.tt("DVE", pm, pt, nmax, ALU.mult)
                m2p = g_t[:, 1, 4:5]
                t.red("DVE", m2p, pm, ALU.max)
                keep = g_t[:, 7, :]
                t.ts("DVE", keep, pt, m2p, None, ALU.is_ge)
                t.tt("DVE", comb[:, sa, :], pt, keep, ALU.mult)

            for sa in range(SP):
                for db in range(4):
                    p = opp[db % 2]
                    for hc in range(NCC):
                        t.mm(p[:, :], oT[:, hc, sa * 128:(sa + 1) * 128],
                             wo_all[:, db, hc, :], start=(hc == 0),
                             stop=(hc == NCC - 1), inc=(hc == NCC - 1))
                    t.tt("DVE", attn_q[:, sa % 2, db * 512:(db + 1) * 512],
                         p[:, :], bo_bc[:, db * 512:(db + 1) * 512], ALU.add)
                chain_a(sa)
                if sa > 0:
                    chain_b(sa - 1)
            chain_b(SP - 1)
            if "x2" in dbg:
                t.dma("ACTQ", dbg_aps["d_comb"], comb[:, :, :])

        o_scope.close()
        t.barrier()

        y_scope = ExitStack()
        y = y_scope.enter_context(nc.sbuf_tensor("y_sb", [128, SP, D], F32))
        with ExitStack() as ph:
            e = ph.enter_context
            h_bf = e(nc.sbuf_tensor("h_bf", [128, NIC, 1024], BF16))
            w12s = e(nc.sbuf_tensor("w12s", [128, 2, D], F32))
            w1b = e(nc.sbuf_tensor("w1b", [128, 2, NCC, 128], BF16))
            w2b = e(nc.sbuf_tensor("w2b", [128, 2, NCC, 128], BF16))
            w3st = e(nc.sbuf_tensor("w3st", [128, 4, 512], F32))
            w3b = e(nc.sbuf_tensor("w3b", [128, 2, NIC, 512], BF16))
            s1b = e(nc.sbuf_tensor("s1b", [128, 2, 512], BF16))
            h1p = [e(nc.psum_tensor(f"h1p{i}", [128, 512], F32)) for i in range(2)]
            h2p = [e(nc.psum_tensor(f"h2p{i}", [128, 512], F32)) for i in range(2)]
            eop = [e(nc.psum_tensor(f"eop{i}", [128, 512], F32)) for i in range(2)]
            for pt in h1p + h2p + eop:
                warm(pt)

            def load12(u, ic):
                rp = ic % 2
                t.dma("SPQ", w12s[:, 0, :], fw1[u, ic])
                t.copy("POOL",
                       w1b[:, rp, :, :].rearrange("p c x -> p (c x)"),
                       w12s[:, 0, :])
                t.dma("SPQ", w12s[:, 1, :], fw2[u, ic])
                t.copy("POOL",
                       w2b[:, rp, :, :].rearrange("p c x -> p (c x)"),
                       w12s[:, 1, :])

            def load3(u, cb, ic):
                t.dma("SPQ", w3st[:, ic % 4, :], fw3[u, cb, ic])
                t.copy("POOL", w3b[:, cb % 2, ic, :], w3st[:, ic % 4, :])

            load12(0, 0)
            for u in range(NU):
                for ic in range(NIC):
                    rp = ic % 2
                    if ic + 1 < NIC:
                        load12(u, ic + 1)
                    for hf in range(2):
                        for cc in range(NCC):
                            t.mm(h1p[hf][:, :], w1b[:, rp, cc, :],
                                 ln2t[:, cc, hf * 512:(hf + 1) * 512],
                                 start=(cc == 0), stop=(cc == NCC - 1),
                                 inc=(cc == NCC - 1))
                        for cc in range(NCC):
                            t.mm(h2p[hf][:, :], w2b[:, rp, cc, :],
                                 ln2t[:, cc, hf * 512:(hf + 1) * 512],
                                 start=(cc == 0), stop=(cc == NCC - 1),
                                 inc=(cc == NCC - 1))
                        t.act(s1b[:, hf, :], h1p[hf][:, :], AF.Silu)
                        t.tt("DVE", h_bf[:, ic, hf * 512:(hf + 1) * 512],
                             h2p[hf][:, :], s1b[:, hf, :], ALU.mult)
                for ic in range(NIC):
                    load3(u, 0, ic)
                for cb in range(4):
                    wp = cb % 2
                    if cb + 1 < 4:
                        for ic in range(NIC):
                            load3(u, cb + 1, ic)
                    elif u + 1 < NU:
                        load12(u + 1, 0)
                    for sa in range(SP):
                        p = eop[sa % 2]
                        for ic in range(NIC):
                            t.mm(p[:, :], h_bf[:, ic, sa * 128:(sa + 1) * 128],
                                 w3b[:, wp, ic, :], start=(ic == 0),
                                 stop=(ic == NIC - 1), inc=(ic == NIC - 1))
                        ysl = y[:, sa, cb * 512:(cb + 1) * 512]
                        if u == 0:
                            t.copy("DVE", ysl, p[:, :])
                        elif u == 1:
                            t.tt("DVE", ysl, p[:, :], ysl, ALU.add)
                        else:
                            t.stt("DVE", ysl, p[:, :], comb[:, sa, u - 2:u - 1],
                                  ysl, ALU.mult, ALU.add)

        t.barrier()

        with ExitStack() as p7:
            e7 = p7.enter_context
            x2r = e7(nc.sbuf_tensor("x2r", [128, 2, D], F32))
            outr = e7(nc.sbuf_tensor("outr", [128, 2, D], F32))
            t.dma("SPQ", x2r[:, 0, :], x2_d[0])
            t.dma("SPQ", x2r[:, 1, :], x2_d[1])
            for sa in range(SP):
                ysl = y[:, sa, :]
                ln_stats(ysl, sa % 2)
                o = outr[:, sa % 2, :]
                t.ts("DVE", o, ysl, mv[:, sa % 2, 0:1],
                     rstd[:, sa % 2:sa % 2 + 1], ALU.subtract, ALU.mult)
                t.tt("DVE", o, o, x2r[:, sa % 2, :], ALU.add)
                t.dma("ACTQ", out_d[sa], o)
                if sa + 2 < SP:
                    t.dma("SPQ", x2r[:, sa % 2, :], x2_d[sa + 2])
        y_scope.close()

        t.finish()
    return nc, dbg_aps



def _prep_shared(L0_attn_w_o, L0_attn_b_o, L0_ffn_gate_w,
                 L0_ffn_w1, L0_ffn_w2, L0_ffn_w3,
                 L0_ffn_sw1, L0_ffn_sw2, L0_ffn_sw3):
    wo_p = np.ascontiguousarray(
        L0_attn_w_o.reshape(NCC, 128, 4, 512).transpose(2, 1, 0, 3))
    bo_r = np.ascontiguousarray(L0_attn_b_o.reshape(1, D))
    gwt_p = np.ascontiguousarray(
        L0_ffn_gate_w.T.reshape(NCC, 128, 8).transpose(1, 0, 2))

    def prep_w12(w_sh, w_ex):
        sh = w_sh.T.reshape(NCC, 128, 2, NIC, 128).transpose(2, 3, 1, 0, 4)
        ex = np.stack([w_ex[ei].T.reshape(NCC, 128, NIC, 128).transpose(2, 1, 0, 3)
                       for ei in range(E)])
        return np.ascontiguousarray(
            np.concatenate([sh, ex], axis=0).reshape(NU, NIC, 128, D))

    fw1 = prep_w12(L0_ffn_sw1, L0_ffn_w1)
    fw2 = prep_w12(L0_ffn_sw2, L0_ffn_w2)

    sh3 = L0_ffn_sw3.T.reshape(2, NIC, 128, 4, 512).transpose(0, 3, 1, 2, 4)
    ex3 = np.stack([L0_ffn_w3[ei].T.reshape(NIC, 128, 4, 512).transpose(2, 0, 1, 3)
                    for ei in range(E)])
    fw3 = np.ascontiguousarray(np.concatenate([sh3, ex3], axis=0))

    masks = np.zeros((128, 4, 512), np.float32)
    for j in range(4):
        for ki in range(128):
            masks[ki, j, :128 * j + ki] = NEG
    masks = masks.astype(ml_dtypes.bfloat16)
    ident = np.eye(128, dtype=ml_dtypes.bfloat16)
    ones_b = np.ones((128, 128), ml_dtypes.bfloat16)
    ones_f = np.ones((128, 128), np.float32)
    return dict(wo_p=wo_p, bo_r=bo_r, gwt_p=gwt_p, fw1=fw1, fw2=fw2, fw3=fw3,
                masks=masks, ident=ident, ones_b=ones_b, ones_f=ones_f)


def _prep_core(c, x, w_q, b_q, w_k, b_k, w_v, b_v):
    own = np.concatenate([x[c * CH:(c + 1) * CH],
                          x[(15 - c) * CH:(16 - c) * CH]], axis=0)
    x_own = np.ascontiguousarray(own.reshape(SP, 128, D))
    cols = slice(256 * c, 256 * (c + 1))

    def wprep(w):
        return np.ascontiguousarray(
            w[:, cols].reshape(NCC, 128, 256).transpose(1, 0, 2))

    bq = b_q[cols].reshape(2, 128).T
    bk = b_k[cols].reshape(2, 128).T
    bqk = np.ascontiguousarray(
        np.stack([bq[:, 0], bq[:, 1], bk[:, 0], bk[:, 1]], axis=1))
    bv_m = np.ascontiguousarray(b_v[cols].reshape(1, 256))
    oidx = np.zeros((128, 32), np.int32)
    q1 = 15 - c
    piece = 0 if q1 < 12 else 1
    within = q1 - 8 - 4 * piece
    for cc in range(NCC):
        for p in range(128):
            oidx[p, cc * 2 + 0] = (cc * 128 + p) * 8 + c
            oidx[p, cc * 2 + 1] = piece * 8192 + (cc * 128 + p) * 4 + within
    return dict(x_own=x_own, wq_m=wprep(w_q), wk_m=wprep(w_k), wv_m=wprep(w_v),
                bqk_m=bqk, bv_m=bv_m, oidx=oidx)


_BUILD_CACHE = {}


def kernel(L0_attn_query, L0_attn_w_q, L0_attn_b_q, L0_attn_w_k, L0_attn_b_k,
           L0_attn_w_v, L0_attn_b_v, L0_attn_w_o, L0_attn_b_o, L0_ffn_gate_w,
           L0_ffn_w1, L0_ffn_w2, L0_ffn_w3, L0_ffn_sw1, L0_ffn_sw2, L0_ffn_sw3,
           _debug_outputs=(), _return_raw=False):
    x = np.asarray(L0_attn_query, np.float32).reshape(T, D)
    shared = _prep_shared(np.asarray(L0_attn_w_o, np.float32),
                          np.asarray(L0_attn_b_o, np.float32),
                          np.asarray(L0_ffn_gate_w, np.float32),
                          np.asarray(L0_ffn_w1, np.float32),
                          np.asarray(L0_ffn_w2, np.float32),
                          np.asarray(L0_ffn_w3, np.float32),
                          np.asarray(L0_ffn_sw1, np.float32),
                          np.asarray(L0_ffn_sw2, np.float32),
                          np.asarray(L0_ffn_sw3, np.float32))
    in_maps = []
    for c in range(NCORE):
        m = dict(shared)
        m.update(_prep_core(c, x,
                            np.asarray(L0_attn_w_q, np.float32),
                            np.asarray(L0_attn_b_q, np.float32),
                            np.asarray(L0_attn_w_k, np.float32),
                            np.asarray(L0_attn_b_k, np.float32),
                            np.asarray(L0_attn_w_v, np.float32),
                            np.asarray(L0_attn_b_v, np.float32)))
        in_maps.append(m)

    key = tuple(sorted(_debug_outputs))
    if key not in _BUILD_CACHE:
        _BUILD_CACHE[key] = build(_debug_outputs)
    nc, dbg_aps = _BUILD_CACHE[key]

    res = run_bass_kernel_spmd(nc, in_maps, core_ids=list(range(NCORE)))
    if _return_raw:
        return res
    full = np.empty((T, D), np.float32)
    for c in range(NCORE):
        o = res.results[c]["out"].reshape(SP * 128, D)
        full[c * CH:(c + 1) * CH] = o[:CH]
        full[(15 - c) * CH:(16 - c) * CH] = o[CH:]
    return full.reshape(1, T, D)


# version 18 (final)
# speedup vs baseline: 1.1570x; 1.0076x over previous
import numpy as np
import ml_dtypes
from contextlib import ExitStack

import concourse.bass as bass
import concourse.mybir as mybir
from concourse.bass_utils import run_bass_kernel_spmd

F32 = mybir.dt.float32
BF16 = mybir.dt.bfloat16
I32 = mybir.dt.int32
ALU = mybir.AluOpType
AF = mybir.ActivationFunctionType
AX = mybir.AxisListType

T = 8192
D = 2048
NH = 16
HS = 128
NCORE = 8
CH = 512
NCHK = 16
SP = 8
E = 8
IM = 1408
NIC = 11
ISH = 2816
NU = 10
NCC = 16
EPS = 1e-5
ISR = 0.08838834764831845
NEG = -1.0e9


def tb_of_chunk(c):
    return 2 * c if c < 8 else 2 * (15 - c) + 1



def _rect(ap):
    t = ap.tensor
    name = t.name
    try:
        off = int(ap.offset)
    except Exception:
        return (name, 0, 1 << 20, 0, 1 << 60)
    dims = list(ap.ap)
    if not dims:
        return (name, 0, 1 << 20, off, off + 1)
    space = str(ap.space)
    asz = mybir.dt.size(ap.dtype)
    if "DRAM" in space:
        lo = off
        hi = off
        for st, cnt in dims:
            if st >= 0:
                hi += st * (cnt - 1)
            else:
                lo += st * (cnt - 1)
        return (name, 0, 1 << 20, lo * asz, (hi + 1) * asz)
    row = 1
    for s in t.shape[1:]:
        row *= s
    tsz = mybir.dt.size(t.dtype)
    if tsz != asz:
        row = row * tsz // asz
    p0 = off // row
    f_lo = off % row
    f_hi = f_lo
    nump = 1
    for st, cnt in dims:
        if st != 0 and st % row == 0:
            nump = max(nump, (cnt - 1) * (st // row) + 1)
        else:
            if st >= 0:
                f_hi += st * (cnt - 1)
            else:
                f_lo += st * (cnt - 1)
    if "PSUM" in space:
        return (name, 0, 128, 0, 1 << 30)
    return (name, p0, p0 + nump, f_lo * asz, (f_hi + 1) * asz)


def _overlap(a, b):
    return (a[0] == b[0]) and not (
        a[2] <= b[1] or b[2] <= a[1] or a[4] <= b[3] or b[4] <= a[3]
    )


def _contains(a, b):
    return (a[0] == b[0]) and a[1] <= b[1] and a[2] >= b[2] and a[3] <= b[3] and a[4] >= b[4]


class Trk:
    def __init__(self, nc):
        self.nc = nc
        self.streams = {}
        self.cnt = {}
        self.waited = {}
        self.writes = {}
        self.reads = {}
        for name, eng, ekey, amt in [
            ("TE", nc.tensor, "te", 1),
            ("DVE", nc.vector, "dve", 1),
            ("ACT", nc.scalar, "act", 1),
            ("POOL", nc.gpsimd, "pool", 1),
            ("SPQ", nc.sync, "sp", 16),
            ("ACTQ", nc.scalar, "act", 16),
            ("POOLQ", nc.gpsimd, "pool", 16),
        ]:
            sem = nc.alloc_semaphore(f"trk_{name}")
            self.streams[name] = (eng, ekey, sem, amt)
            self.cnt[name] = 0

    def _deps_for(self, rect, want_writes, want_reads):
        deps = []
        if want_writes:
            for r, s, v in self.writes.get(rect[0], ()):
                if _overlap(rect, r):
                    deps.append((s, v))
        if want_reads:
            for r, s, v in self.reads.get(rect[0], ()):
                if _overlap(rect, r):
                    deps.append((s, v))
        return deps

    def _emit_waits(self, stream, deps):
        eng, ekey, ssem, _ = self.streams[stream]
        best = {}
        selfmax = 0
        for s, v in deps:
            if s == stream:
                if stream not in ("TE", "SPQ", "ACTQ", "POOLQ"):
                    selfmax = max(selfmax, v)
                continue
            _, dekey, dsem, _ = self.streams[s]
            if s == "TE" and ekey == "te":
                continue
            if dekey == ekey and dsem.name == ssem.name:
                continue
            key = (ekey, dsem.name)
            if self.waited.get(key, 0) >= v:
                continue
            cur = best.get(dsem.name)
            if cur is None or v > cur[1]:
                best[dsem.name] = (dsem, v, key)
        for dsem, v, key in best.values():
            eng.wait_ge(dsem, v)
            self.waited[key] = v
        if selfmax > 0:
            key = (ekey + "#self", ssem.name)
            if self.waited.get(key, 0) < selfmax:
                eng.wait_ge(ssem, selfmax)
                self.waited[key] = selfmax

    def _record(self, rect, stream, value, is_write):
        name = rect[0]
        if is_write:
            wl = self.writes.setdefault(name, [])
            wl[:] = [e for e in wl if not _contains(rect, e[0])]
            wl.append((rect, stream, value))
            rl = self.reads.setdefault(name, [])
            rl[:] = [e for e in rl if not _contains(rect, e[0])]
        else:
            rl = self.reads.setdefault(name, [])
            rl[:] = [e for e in rl
                     if not (_contains(rect, e[0]) and e[1] == stream)]
            rl.append((rect, stream, value))

    def op(self, stream, fn, reads=(), writes=(), inc=True):
        eng, ekey, sem, amt = self.streams[stream]
        rrects = [_rect(a) for a in reads]
        wrects = [_rect(a) for a in writes]
        deps = []
        for r in rrects:
            deps += self._deps_for(r, True, False)
        for r in wrects:
            deps += self._deps_for(r, True, True)
        self._emit_waits(stream, deps)
        inst = fn(eng)
        if inc:
            inst.then_inc(sem, amt)
            self.cnt[stream] += amt
        val = self.cnt[stream] + (0 if inc else amt)
        for r in rrects:
            self._record(r, stream, val, False)
        for r in wrects:
            self._record(r, stream, val, True)
        return inst

    def dma(self, queue, out, in_, **kw):
        return self.op(queue, lambda e: e.dma_start(out=out, in_=in_, **kw),
                       reads=[in_], writes=[out])

    def mm(self, out, lhsT, rhs, start=True, stop=True, inc=True, **kw):
        return self.op("TE",
                       lambda e: e.matmul(out, lhsT, rhs, start=start, stop=stop, **kw),
                       reads=[lhsT, rhs], writes=[out], inc=inc)

    def tr(self, out, in_, ident, inc=True):
        return self.op("TE", lambda e: e.transpose(out, in_, ident),
                       reads=[in_, ident], writes=[out], inc=inc)

    def act(self, out, in_, func, bias=0.0, scale=1.0, accum_out=None):
        reads = [in_]
        writes = [out]
        if hasattr(bias, "tensor"):
            reads.append(bias)
        if accum_out is not None:
            writes = [out, accum_out]
            return self.op("ACT",
                           lambda e: e.activation(out, in_, func, bias=bias,
                                                  scale=scale, accum_out=accum_out),
                           reads=reads, writes=writes)
        return self.op("ACT",
                       lambda e: e.activation(out, in_, func, bias=bias, scale=scale),
                       reads=reads, writes=writes)

    def copy(self, stream, out, in_):
        return self.op(stream, lambda e: e.tensor_copy(out, in_),
                       reads=[in_], writes=[out])

    def tt(self, stream, out, in0, in1, alu):
        return self.op(stream, lambda e: e.tensor_tensor(out, in0, in1, alu),
                       reads=[in0, in1], writes=[out])

    def ts(self, stream, out, in0, s1, s2, op0, op1=None):
        reads = [in0]
        if hasattr(s1, "tensor"):
            reads.append(s1)
        if s2 is not None and hasattr(s2, "tensor"):
            reads.append(s2)
        if op1 is None:
            return self.op(stream, lambda e: e.tensor_scalar(out, in0, s1, s2, op0),
                           reads=reads, writes=[out])
        return self.op(stream, lambda e: e.tensor_scalar(out, in0, s1, s2, op0, op1),
                       reads=reads, writes=[out])

    def stt(self, stream, out, in0, scalar, in1, op0, op1):
        reads = [in0, in1]
        if hasattr(scalar, "tensor"):
            reads.append(scalar)
        return self.op(stream,
                       lambda e: e.scalar_tensor_tensor(out, in0, scalar, in1, op0, op1),
                       reads=reads, writes=[out])

    def red(self, stream, out, in_, alu):
        return self.op(stream, lambda e: e.tensor_reduce(out, in_, AX.X, alu),
                       reads=[in_], writes=[out])

    def recip(self, out, in_):
        return self.op("DVE", lambda e: e.reciprocal(out, in_),
                       reads=[in_], writes=[out])

    def bn(self, out, in_, nchunks):
        n = in_.shape[-1] // nchunks
        for i in range(nchunks):
            sl = in_[:, i * n:(i + 1) * n]
            self.op("DVE", lambda e, i=i, sl=sl: e.bn_stats(
                self._bn_buf[:, i, :], sl),
                reads=[sl], writes=[self._bn_buf[:, i, :]])
        self.op("DVE", lambda e: e.bn_aggr(out, self._bn_buf[:, 0:nchunks, :]),
                reads=[self._bn_buf[:, 0:nchunks, :]], writes=[out])

    def collective(self, kind, aluop, ins, outs, replica_groups):
        sem = self.nc.alloc_semaphore(f"cc_{len(self.streams)}")
        eng, ekey, _, _ = self.streams["POOL"]
        deps = []
        for a in ins:
            deps += self._deps_for(_rect(a), True, False)
        for a in outs:
            deps += self._deps_for(_rect(a), True, True)
        self._emit_waits("POOL", deps)
        inst = eng.collective_compute(kind, aluop, replica_groups=replica_groups,
                                      ins=ins, outs=outs)
        inst.then_inc(sem)
        sname = f"CC{len(self.streams)}"
        self.streams[sname] = (eng, "pool", sem, 1)
        self.cnt[sname] = 1
        for a in outs:
            self._record(_rect(a), sname, 1, True)
        for a in ins:
            self._record(_rect(a), sname, 1, False)
        return inst

    def indirect(self, out, in_, idx):
        from concourse.bass import IndirectOffsetOnAxis
        return self.op("POOLQ",
                       lambda e: e.indirect_dma_start(
                           out, None, in_, IndirectOffsetOnAxis(idx, 0)),
                       reads=[in_, idx], writes=[out])

    def barrier(self):
        snap = [(sname, tup[2], self.cnt.get(sname, 0))
                for sname, tup in self.streams.items()
                if self.cnt.get(sname, 0) > 0 and not sname.startswith("CC")]
        for wname in ("TE", "DVE", "ACT", "POOL", "SPQ"):
            weng, wekey, _, _ = self.streams[wname]
            for sname, sem, v in snap:
                key = (wekey, sem.name)
                if self.waited.get(key, 0) < v:
                    weng.wait_ge(sem, v)
                    self.waited[key] = v

    def finish(self):
        eng, ekey, _, _ = self.streams["POOL"]
        for sname, (seng, sekey, sem, amt) in list(self.streams.items()):
            v = self.cnt.get(sname, 0)
            if v > 0:
                key = (ekey, sem.name)
                if self.waited.get(key, 0) < v:
                    eng.wait_ge(sem, v)
                    self.waited[key] = v



def build(debug_outputs=()):
    nc = bass.Bass("TRN2", target_bir_lowering=False)
    dbg = set(debug_outputs)

    x_own = nc.dram_tensor("x_own", [SP, 128, D], F32, kind="ExternalInput").ap()
    wq_m = nc.dram_tensor("wq_m", [128, NCC, 256], F32, kind="ExternalInput").ap()
    wk_m = nc.dram_tensor("wk_m", [128, NCC, 256], F32, kind="ExternalInput").ap()
    wv_m = nc.dram_tensor("wv_m", [128, NCC, 256], F32, kind="ExternalInput").ap()
    bqk_m = nc.dram_tensor("bqk_m", [128, 4], F32, kind="ExternalInput").ap()
    bv_m = nc.dram_tensor("bv_m", [1, 256], F32, kind="ExternalInput").ap()
    bo_r = nc.dram_tensor("bo_r", [1, D], F32, kind="ExternalInput").ap()
    wo_p = nc.dram_tensor("wo_p", [4, 128, NCC, 512], F32, kind="ExternalInput").ap()
    gwt_p = nc.dram_tensor("gwt_p", [128, NCC, 8], F32, kind="ExternalInput").ap()
    fw1 = nc.dram_tensor("fw1", [NU, NIC, 128, D], F32, kind="ExternalInput").ap()
    fw2 = nc.dram_tensor("fw2", [NU, NIC, 128, D], F32, kind="ExternalInput").ap()
    fw3 = nc.dram_tensor("fw3", [NU, 4, NIC, 128, 512], F32, kind="ExternalInput").ap()
    masks_d = nc.dram_tensor("masks", [128, 4, 512], BF16, kind="ExternalInput").ap()
    ident_d = nc.dram_tensor("ident", [128, 128], BF16, kind="ExternalInput").ap()
    ones_b_d = nc.dram_tensor("ones_b", [128, 128], BF16, kind="ExternalInput").ap()
    ones_f_d = nc.dram_tensor("ones_f", [128, 128], F32, kind="ExternalInput").ap()
    oidx_d = nc.dram_tensor("oidx", [128, 32], I32, kind="ExternalInput").ap()

    out_d = nc.dram_tensor("out", [SP, 128, D], F32, kind="ExternalOutput").ap()

    ln1_own = nc.dram_tensor("ln1_own", [2, D, 512], BF16, kind="Internal").ap()
    ln1_alla = nc.dram_tensor("ln1_alla", [NCORE * D, 512], BF16, kind="Internal", addr_space="Shared").ap()
    ln1_allb = nc.dram_tensor("ln1_allb", [NCORE * D, 512], BF16, kind="Internal", addr_space="Shared").ap()
    ccwarm_i = nc.dram_tensor("ccwarm_i", [16, 16], BF16, kind="Internal").ap()
    ccwarm_o = nc.dram_tensor("ccwarm_o", [128, 16], BF16, kind="Internal", addr_space="Shared").ap()
    x2_d = nc.dram_tensor("x2_d", [SP, 128, D], F32, kind="Internal").ap()

    o_own_a = nc.dram_tensor("o_own_a", [256, T // 2], BF16, kind="Internal").ap()
    o_own_b = nc.dram_tensor("o_own_b", [256, T // 4], BF16, kind="Internal").ap()
    o_own_c = nc.dram_tensor("o_own_c", [256, T // 4], BF16, kind="Internal").ap()
    o_all_a = nc.dram_tensor("o_all_a", [D, T // 2], BF16, kind="Internal", addr_space="Shared").ap()
    o_all_bc = nc.dram_tensor("o_all_bc", [2, D, T // 4], BF16, kind="Internal", addr_space="Shared").ap()

    dbg_aps = {}
    if "ln1_all" in dbg:
        dbg_aps["d_ln1"] = nc.dram_tensor("d_ln1", [NCORE * D, 1024], BF16,
                                          kind="ExternalOutput").ap()
    if "kqv" in dbg:
        dbg_aps["d_k"] = nc.dram_tensor("d_k", [128, 2, NCHK, 512], BF16,
                                        kind="ExternalOutput").ap()
        dbg_aps["d_q"] = nc.dram_tensor("d_q", [128, 2, NCHK, 512], BF16,
                                        kind="ExternalOutput").ap()
        dbg_aps["d_v"] = nc.dram_tensor("d_v", [128, 64, 256], BF16,
                                        kind="ExternalOutput").ap()
    if "o_all" in dbg:
        dbg_aps["d_o"] = nc.dram_tensor("d_o", [D, T], BF16,
                                        kind="ExternalOutput").ap()
    if "x2" in dbg:
        dbg_aps["d_x2"] = nc.dram_tensor("d_x2", [SP, 128, D], F32,
                                         kind="ExternalOutput").ap()
        dbg_aps["d_comb"] = nc.dram_tensor("d_comb", [128, SP, 8], F32,
                                           kind="ExternalOutput").ap()

    t = Trk(nc)
    RG = [list(range(NCORE))]

    with ExitStack() as top:
        et = top.enter_context
        ident = et(nc.sbuf_tensor("ident_sb", [128, 128], BF16))
        ones_b = et(nc.sbuf_tensor("ones_b_sb", [128, 128], BF16))
        ones_f = et(nc.sbuf_tensor("ones_f_sb", [128, 128], F32))
        bqk = et(nc.sbuf_tensor("bqk_sb", [128, 4], F32))
        bv_row = et(nc.sbuf_tensor("bv_row", [1, 256], F32))
        bv_bc = et(nc.sbuf_tensor("bv_bc", [128, 256], F32))
        gwt = et(nc.sbuf_tensor("gwt_sb", [128, NCC, 8], F32))
        gw_hi = et(nc.sbuf_tensor("gw_hi", [128, NCC, 8], BF16))
        gw_lo = et(nc.sbuf_tensor("gw_lo", [128, NCC, 8], BF16))
        shavg = et(nc.sbuf_tensor("shavg", [128, 1], F32))
        sh_tmp = et(nc.sbuf_tensor("sh_tmp", [128, 2], F32))
        sh_row = et(nc.sbuf_tensor("sh_row", [1, 2], F32))
        oidx = et(nc.sbuf_tensor("oidx_sb", [128, 32], I32))
        comb = et(nc.sbuf_tensor("comb_sb", [128, SP, 8], F32))
        bnb = et(nc.sbuf_tensor("bn_buf", [128, 4, 6], F32))
        mv = et(nc.sbuf_tensor("mv_sb", [128, 2, 2], F32))
        rstd = et(nc.sbuf_tensor("rstd_sb", [128, 2], F32))
        eps_t = et(nc.sbuf_tensor("eps_t", [128, 1], F32))
        t._bn_buf = bnb
        t.op("DVE", lambda e: e.memset(eps_t[:, :], EPS), writes=[eps_t[:, :]])

        o_scope = ExitStack()
        oT = o_scope.enter_context(nc.sbuf_tensor("oT_sb", [128, NCC, 1024], BF16, side="right"))
        w_scope = ExitStack()
        ew = w_scope.enter_context
        wst = ew(nc.sbuf_tensor("w_stage", [128, 8, 256], F32))
        wqb = ew(nc.sbuf_tensor("wq_b", [128, NCC, 256], BF16))
        wkb = ew(nc.sbuf_tensor("wk_b", [128, NCC, 256], BF16))
        wvb = ew(nc.sbuf_tensor("wv_b", [128, NCC, 256], BF16))

        t.dma("ACTQ", ident[:, :], ident_d)
        t.dma("ACTQ", ones_b[:, :], ones_b_d)
        t.dma("ACTQ", ones_f[:, :], ones_f_d)
        t.dma("ACTQ", bqk[:, :], bqk_m)
        t.dma("ACTQ", bv_row[:, :], bv_m)
        t.dma("ACTQ", gwt[:, :, :], gwt_p)
        t.dma("ACTQ", oidx[:, :], oidx_d)
        t.collective("AllGather", ALU.bypass, [ccwarm_i], [ccwarm_o], RG)


        def warm(pt):
            ap = pt[:, :]
            if ap.dtype == BF16:
                ap = ap.bitcast(F32)
            t.op("DVE", lambda e, ap=ap: e.memset(ap, 0.0), writes=[ap])

        def ln_stats(src, slot):
            t.bn(mv[:, slot, :], src, 4)
            t.act(rstd[:, slot:slot + 1], mv[:, slot, 1:2], AF.Ln, bias=eps_t[:, :])
            t.act(rstd[:, slot:slot + 1], rstd[:, slot:slot + 1], AF.Exp, scale=-0.5)

        with ExitStack() as ph:
            e = ph.enter_context
            x_ring = e(nc.sbuf_tensor("x_ring", [128, 2, D], F32))
            ln1_tok = e(nc.sbuf_tensor("ln1_tok", [128, SP, D], BF16))
            tr_st = e(nc.sbuf_tensor("tr_st", [128, 2, 1024], BF16))
            cps = e(nc.psum_tensor("cps", [128, 512], F32))
            trp = [e(nc.psum_tensor(f"trp{i}", [128, 1024], BF16)) for i in range(2)]
            for pt in [cps] + trp:
                warm(pt)

            t.mm(cps[:, 0:256], ones_f[0:1, :], bv_row[0:1, :])
            t.copy("DVE", bv_bc[:, :], cps[:, 0:256])
            t.stt("DVE", sh_tmp[:, :], bqk[:, 0:2], -ISR, bqk[:, 2:4],
                  ALU.mult, ALU.mult)
            t.mm(cps[0:1, 0:2], ones_f[:, 0:1], sh_tmp[:, :])
            t.copy("DVE", sh_row[0:1, :], cps[0:1, 0:2])
            t.red("DVE", sh_row[0:1, 0:1], sh_row[0:1, :], ALU.add)
            t.ts("DVE", sh_row[0:1, 0:1], sh_row[0:1, 0:1], 0.5, None, ALU.mult)
            t.mm(cps[:, 0:1], ones_f[0:1, :], sh_row[0:1, 0:1])
            t.copy("DVE", shavg[:, :], cps[:, 0:1])
            t.copy("DVE", gw_hi[:, :, :], gwt[:, :, :])
            t.tt("DVE", gw_lo[:, :, :], gwt[:, :, :], gw_hi[:, :, :], ALU.subtract)

            for s in range(SP):
                xs = x_ring[:, s % 2, :]
                t.dma("SPQ", xs, x_own[s])
                ln_stats(xs, s % 2)
                t.ts("DVE", ln1_tok[:, s, :], xs,
                     mv[:, s % 2, 0:1], rstd[:, s % 2:s % 2 + 1],
                     ALU.subtract, ALU.mult)
            for h in range(2):
                for cc in range(NCC):
                    pb = trp[cc % 2]
                    for si in range(4):
                        sa = h * 4 + si
                        t.tr(pb[:, si * 128:(si + 1) * 128],
                             ln1_tok[:, sa, cc * 128:(cc + 1) * 128], ident[:, :],
                             inc=(si == 3))
                    st = tr_st[:, cc % 2, 0:512]
                    t.copy("DVE", st, pb[:, 0:512])
                    t.dma("ACTQ", ln1_own[h, cc * 128:(cc + 1) * 128, :], st)
                t.collective("AllGather", ALU.bypass, [ln1_own[h]],
                             [ln1_alla if h == 0 else ln1_allb], RG)
            for wsrc, wdst in ((wq_m, wqb), (wk_m, wkb), (wv_m, wvb)):
                for hh in range(2):
                    t.dma("SPQ", wst[:, :, :], wsrc[:, hh * 8:(hh + 1) * 8, :])
                    t.copy("DVE", wdst[:, hh * 8:(hh + 1) * 8, :], wst[:, :, :])

        t.barrier()

        kqv_scope = ExitStack()
        e = kqv_scope.enter_context
        masks = e(nc.sbuf_tensor("masks_sb", [128, 4, 512], BF16))
        k_res = e(nc.sbuf_tensor("k_res", [128, 2, NCHK, 512], BF16))
        q_res = e(nc.sbuf_tensor("q_res", [128, 2, NCHK, 512], BF16))
        v_res = e(nc.sbuf_tensor("v_res", [128, 64, 256], BF16))

        t.dma("ACTQ", masks[:, :, :], masks_d)
        with ExitStack() as ph:
            e = ph.enter_context
            lnr = e(nc.sbuf_tensor("lnall_ring", [128, 2, NCC, 512], BF16))
            kp = [e(nc.psum_tensor(f"kp{i}", [128, 512], F32)) for i in range(2)]
            qp = [e(nc.psum_tensor(f"qp{i}", [128, 512], F32)) for i in range(2)]
            vp = [e(nc.psum_tensor(f"vp{i}", [128, 512], F32)) for i in range(2)]
            for pt in kp + qp + vp:
                warm(pt)

            def lnall_block(tb):
                src = ln1_alla if tb % 2 == 0 else ln1_allb
                r = tb // 2
                return src.rearrange("(r cc p) q -> p r cc q", p=128, cc=NCC)[:, r]

            tb_order = list(range(0, NCHK, 2)) + list(range(1, NCHK, 2))
            t.dma("SPQ", lnr[:, 0, :, :], lnall_block(tb_order[0]))
            for ti in range(NCHK):
                tb = tb_order[ti]
                if ti + 1 < NCHK:
                    t.dma("SPQ", lnr[:, (ti + 1) % 2, :, :],
                          lnall_block(tb_order[ti + 1]))
                ln = lnr[:, ti % 2, :, :]
                for hp in range(2):
                    for cc in range(NCC):
                        t.mm(kp[hp][:, :], wkb[:, cc, hp * 128:(hp + 1) * 128],
                             ln[:, cc, :], start=(cc == 0), stop=(cc == NCC - 1),
                             inc=(cc == NCC - 1))
                    t.act(k_res[:, hp, tb, :], kp[hp][:, :], AF.Identity,
                          bias=bqk[:, 2 + hp:3 + hp])
                    for cc in range(NCC):
                        t.mm(qp[hp][:, :], wqb[:, cc, hp * 128:(hp + 1) * 128],
                             ln[:, cc, :], start=(cc == 0), stop=(cc == NCC - 1),
                             inc=(cc == NCC - 1))
                    t.act(q_res[:, hp, tb, :], qp[hp][:, :], AF.Identity,
                          bias=bqk[:, hp:hp + 1])
                for j in range(4):
                    p = vp[j % 2]
                    for cc in range(NCC):
                        t.mm(p[:, 0:256], ln[:, cc, j * 128:(j + 1) * 128],
                             wvb[:, cc, :], start=(cc == 0), stop=(cc == NCC - 1),
                             inc=(cc == NCC - 1))
                    t.tt("DVE", v_res[:, tb * 4 + j, :], p[:, 0:256], bv_bc[:, :],
                         ALU.add)
            if "kqv" in dbg:
                t.dma("ACTQ", dbg_aps["d_k"], k_res[:, :, :, :])
                t.dma("ACTQ", dbg_aps["d_q"], q_res[:, :, :, :])
                t.dma("ACTQ", dbg_aps["d_v"], v_res[:, :, :])

        t.barrier()

        with ExitStack() as ph:
            e = ph.enter_context
            p_ring = e(nc.sbuf_tensor("p_ring", [128, 4, 1024], BF16))
            o_st = e(nc.sbuf_tensor("o_st", [128, 2, 2, 512], BF16))
            rcp = e(nc.sbuf_tensor("rcp_sb", [1, 2, 512], F32))
            r_sb = e(nc.sbuf_tensor("r_sb", [128, 2, 512], F32))
            lg = [e(nc.psum_tensor(f"lg{i}", [128, 1024], F32)) for i in range(2)]
            ops = [e(nc.psum_tensor(f"op{i}", [128, 512], F32)) for i in range(2)]
            sps = [e(nc.psum_tensor(f"sp{i}", [128, 512], F32)) for i in range(2)]
            for pt in lg + ops + sps:
                warm(pt)

            o_flat_a = o_all_a.rearrange("r (k c) -> (r k) c", c=512)
            o_flat_bc = o_all_bc.rearrange("h r (k c) -> (h r k) c", c=512)

            def norm_recip(qb):
                for hp in range(2):
                    t.recip(rcp[0:1, hp, :], sps[hp][0:1, :])

            def norm_store(qb):
                for hp in range(2):
                    t.mm(sps[hp][:, :], ones_f[0:1, :], rcp[0:1, hp, :])
                    t.copy("DVE", r_sb[:, hp, :], sps[hp][:, :])
                    t.tt("DVE", o_st[:, qb % 2, hp, :], ops[hp][:, :],
                         r_sb[:, hp, :], ALU.mult)
                if qb < 8:
                    half, cq = o_own_a, qb
                elif qb < 12:
                    half, cq = o_own_b, qb - 8
                else:
                    half, cq = o_own_c, qb - 12
                t.dma("ACTQ",
                      half.rearrange("(h p) q -> p h q", p=128)[
                          :, :, cq * 512:(cq + 1) * 512],
                      o_st[:, qb % 2, :, :])

            def sums_av(n, qb_n, tc, j, first, last):
                for hp in range(2):
                    phalf = p_ring[:, n % 4, hp * 512:(hp + 1) * 512]
                    t.mm(sps[hp][0:64, :], ones_b[:, 0:64], phalf,
                         start=first, stop=last, inc=False)
                    t.mm(ops[hp][:, :],
                         v_res[:, tc * 4 + j, hp * 128:(hp + 1) * 128],
                         phalf, start=first, stop=last, inc=(hp == 1))

            tiles = []
            for qb in range(NCHK):
                for c in range(qb + 1):
                    for j in range(4):
                        tiles.append((qb, c, j, c == 0 and j == 0,
                                      c == qb and j == 3))
            def fire_norm(qb):
                norm_store(qb)
                if qb == 7:
                    t.collective("AllGather", ALU.bypass,
                                 [o_own_a], [o_all_a], RG)
                    for cc in range(NCC):
                        t.indirect(oT[:, cc, 0:512], o_flat_a,
                                   oidx[:, cc * 2:cc * 2 + 1])
                elif qb == 11:
                    t.collective("AllGather", ALU.bypass,
                                 [o_own_b], [o_all_bc[0]], RG)

            pq = []
            pend_norm = None
            gn = 0
            for (qb, c, j, first, last) in tiles:
                g = lg[gn % 2]
                ps = p_ring[:, gn % 4, :]
                tc = tb_of_chunk(c)
                tq = tb_of_chunk(qb)
                t.mm(g[:, 0:512], k_res[:, 0, tc, j * 128:(j + 1) * 128],
                     q_res[:, 0, tq, :], inc=False)
                t.mm(g[:, 512:1024], k_res[:, 1, tc, j * 128:(j + 1) * 128],
                     q_res[:, 1, tq, :], inc=True)
                if c == qb:
                    gv = g[:, :].rearrange("p (h q) -> p h q", h=2)
                    t.tt("DVE", gv, gv,
                         masks[:, j:j + 1, :].broadcast_to([128, 2, 512]),
                         ALU.add)
                t.act(ps, g[:, :], AF.Exp, bias=shavg[:, :], scale=ISR)
                if pend_norm is not None:
                    fire_norm(pend_norm)
                    pend_norm = None
                if len(pq) >= 2:
                    p = pq.pop(0)
                    sums_av(p[5], p[0], tb_of_chunk(p[1]), p[2], p[3], p[4])
                    if p[4]:
                        norm_recip(p[0])
                        pend_norm = p[0]
                pq.append((qb, c, j, first, last, gn))
                gn += 1
            for p in pq:
                if pend_norm is not None:
                    fire_norm(pend_norm)
                    pend_norm = None
                sums_av(p[5], p[0], tb_of_chunk(p[1]), p[2], p[3], p[4])
                if p[4]:
                    norm_recip(p[0])
                    pend_norm = p[0]
            norm_store(pend_norm)
            t.collective("AllGather", ALU.bypass, [o_own_c], [o_all_bc[1]], RG)
            for cc in range(NCC):
                t.indirect(oT[:, cc, 512:1024], o_flat_bc,
                           oidx[:, cc * 2 + 1:cc * 2 + 2])
            if "o_all" in dbg:
                t.dma("ACTQ", dbg_aps["d_o"][:, 0:T // 2], o_all_a)
                t.dma("ACTQ", dbg_aps["d_o"][:, T // 2:3 * T // 4], o_all_bc[0])
                t.dma("ACTQ", dbg_aps["d_o"][:, 3 * T // 4:T], o_all_bc[1])

        kqv_scope.close()
        w_scope.close()
        t.barrier()

        ln2t = top.enter_context(nc.sbuf_tensor("ln2t", [128, NCC, 1024], BF16))
        with ExitStack() as ph:
            e = ph.enter_context
            attn_q = e(nc.sbuf_tensor("attn_q", [128, 2, D], F32))
            wo_st = e(nc.sbuf_tensor("wo_st", [128, 2, 4, 512], F32))
            wo_all = e(nc.sbuf_tensor("wo_all", [128, 4, NCC, 512], BF16))
            bo_bc = e(nc.sbuf_tensor("bo_bc", [128, D], F32))
            x_r2 = e(nc.sbuf_tensor("x_r2", [128, 2, D], F32))
            hi_r = e(nc.sbuf_tensor("hi_r", [128, 2, D], BF16))
            lo_s = e(nc.sbuf_tensor("lo_s", [128, 2, NCC, 128], BF16))
            g_t = e(nc.sbuf_tensor("g_t", [128, 8, 8], F32))
            nmr_t = e(nc.sbuf_tensor("nmr_t", [128, 2], F32))
            opp = [e(nc.psum_tensor(f"opp{i}", [128, 512], F32)) for i in range(2)]
            tp = e(nc.psum_tensor("tp", [128, 1024], BF16))
            gp = e(nc.psum_tensor("gp", [128, 512], F32))
            for pt in opp + [tp, gp]:
                warm(pt)

            for db in range(4):
                for q4 in range(4):
                    st = wo_st[:, q4 % 2, :, :]
                    t.dma("SPQ", st, wo_p[db, :, q4 * 4:(q4 + 1) * 4, :])
                    t.copy("POOL", wo_all[:, db, q4 * 4:(q4 + 1) * 4, :], st)
            bo_stage = attn_q[0:1, 1, :]
            t.dma("ACTQ", bo_stage, bo_r)
            for b in range(4):
                t.mm(gp[:, :], ones_f[0:1, :], attn_q[0:1, 1, b * 512:(b + 1) * 512])
                t.copy("DVE", bo_bc[:, b * 512:(b + 1) * 512], gp[:, :])

            def chain_a(sa):
                sl = sa % 2
                a = attn_q[:, sl, :]
                xs = x_r2[:, sl, :]
                t.dma("SPQ", xs, x_own[sa])
                ln_stats(a, sl)
                t.ts("DVE", nmr_t[:, sl:sl + 1], mv[:, sl, 0:1],
                     rstd[:, sl:sl + 1], -1.0, ALU.mult, ALU.mult)
                t.act(a, a, AF.Identity, bias=nmr_t[:, sl:sl + 1],
                      scale=rstd[:, sl:sl + 1])
                t.tt("POOL", a, a, xs, ALU.add)
                t.dma("ACTQ", x2_d[sa], a)
                if "x2" in dbg:
                    t.dma("ACTQ", dbg_aps["d_x2"][sa], a)
                ln_stats(a, sl)
                hi = hi_r[:, sl, :]
                tmp = x_r2[:, sl, :]
                t.ts("DVE", nmr_t[:, sl:sl + 1], mv[:, sl, 0:1],
                     rstd[:, sl:sl + 1], -1.0, ALU.mult, ALU.mult)
                t.act(tmp, a, AF.Identity, bias=nmr_t[:, sl:sl + 1],
                      scale=rstd[:, sl:sl + 1])
                t.act(hi, tmp, AF.Copy)
                t.tt("DVE", lo_s[:, sl, :, :],
                     tmp[:, :].rearrange("p (c x) -> p c x", x=128),
                     hi[:, :].rearrange("p (c x) -> p c x", x=128),
                     ALU.subtract)

            def chain_b(sa):
                sl = sa % 2
                hi = hi_r[:, sl, :]
                for ccg in range(4):
                    for c4 in range(4):
                        cc = ccg * 4 + c4
                        t.tr(tp[:, c4 * 256:c4 * 256 + 128],
                             hi[:, cc * 128:(cc + 1) * 128], ident[:, :],
                             inc=False)
                        t.tr(tp[:, c4 * 256 + 128:c4 * 256 + 256],
                             lo_s[:, sl, cc, :], ident[:, :], inc=(c4 == 3))
                    tpv = tp[:, :].rearrange("p (c two q) -> p c two q",
                                             c=4, two=2)
                    t.copy("DVE",
                           ln2t[:, ccg * 4:(ccg + 1) * 4,
                                sa * 128:(sa + 1) * 128],
                           tpv[:, :, 0, :])
                    t.copy("DVE", lo_s[:, sl, ccg * 4:(ccg + 1) * 4, :],
                           tpv[:, :, 1, :])
                for cc in range(NCC):
                    hiT = ln2t[:, cc, sa * 128:(sa + 1) * 128]
                    t.mm(gp[:, 0:8], hiT, gw_hi[:, cc, :],
                         start=(cc == 0), stop=False, inc=False)
                    t.mm(gp[:, 0:8], hiT, gw_lo[:, cc, :],
                         start=False, stop=False, inc=False)
                    t.mm(gp[:, 0:8], lo_s[:, sl, cc, :], gw_hi[:, cc, :],
                         start=False, stop=(cc == NCC - 1),
                         inc=(cc == NCC - 1))
                gl = g_t[:, 0, :]
                t.copy("DVE", gl, gp[:, 0:8])
                m1 = g_t[:, 1, 0:1]
                t.red("DVE", m1, gl, ALU.max)
                t1 = g_t[:, 2, :]
                t.ts("DVE", t1, gl, m1, None, ALU.subtract)
                et_ = g_t[:, 3, :]
                t.act(et_, t1, AF.Exp)
                se = g_t[:, 1, 1:2]
                t.red("DVE", se, et_, ALU.add)
                rse = g_t[:, 1, 2:3]
                t.recip(rse, se)
                pt = g_t[:, 4, :]
                t.ts("DVE", pt, et_, rse, None, ALU.mult)
                m1p = g_t[:, 1, 3:4]
                t.red("DVE", m1p, pt, ALU.max)
                nmax = g_t[:, 5, :]
                t.ts("DVE", nmax, pt, m1p, None, ALU.is_lt)
                pm = g_t[:, 6, :]
                t.tt("DVE", pm, pt, nmax, ALU.mult)
                m2p = g_t[:, 1, 4:5]
                t.red("DVE", m2p, pm, ALU.max)
                keep = g_t[:, 7, :]
                t.ts("DVE", keep, pt, m2p, None, ALU.is_ge)
                t.tt("DVE", comb[:, sa, :], pt, keep, ALU.mult)

            for sa in range(SP):
                for db in range(4):
                    p = opp[db % 2]
                    for hc in range(NCC):
                        t.mm(p[:, :], oT[:, hc, sa * 128:(sa + 1) * 128],
                             wo_all[:, db, hc, :], start=(hc == 0),
                             stop=(hc == NCC - 1), inc=(hc == NCC - 1))
                    t.tt("DVE", attn_q[:, sa % 2, db * 512:(db + 1) * 512],
                         p[:, :], bo_bc[:, db * 512:(db + 1) * 512], ALU.add)
                chain_a(sa)
                if sa > 0:
                    chain_b(sa - 1)
            chain_b(SP - 1)
            if "x2" in dbg:
                t.dma("ACTQ", dbg_aps["d_comb"], comb[:, :, :])

        o_scope.close()
        t.barrier()

        y_scope = ExitStack()
        y = y_scope.enter_context(nc.sbuf_tensor("y_sb", [128, SP, D], F32))
        with ExitStack() as ph:
            e = ph.enter_context
            h_bf = e(nc.sbuf_tensor("h_bf", [128, NIC, 1024], BF16))
            w12s = e(nc.sbuf_tensor("w12s", [128, 2, D], F32))
            w1b = e(nc.sbuf_tensor("w1b", [128, 2, NCC, 128], BF16))
            w2b = e(nc.sbuf_tensor("w2b", [128, 2, NCC, 128], BF16))
            w3st = e(nc.sbuf_tensor("w3st", [128, 4, 512], F32))
            w3b = e(nc.sbuf_tensor("w3b", [128, 2, NIC, 512], BF16))
            s1b = e(nc.sbuf_tensor("s1b", [128, 2, 512], BF16))
            h1p = [e(nc.psum_tensor(f"h1p{i}", [128, 512], F32)) for i in range(2)]
            h2p = [e(nc.psum_tensor(f"h2p{i}", [128, 512], F32)) for i in range(2)]
            eop = [e(nc.psum_tensor(f"eop{i}", [128, 512], F32)) for i in range(2)]
            for pt in h1p + h2p + eop:
                warm(pt)

            def load12(u, ic):
                rp = ic % 2
                t.dma("SPQ", w12s[:, 0, :], fw1[u, ic])
                t.copy("POOL",
                       w1b[:, rp, :, :].rearrange("p c x -> p (c x)"),
                       w12s[:, 0, :])
                t.dma("SPQ", w12s[:, 1, :], fw2[u, ic])
                t.copy("POOL",
                       w2b[:, rp, :, :].rearrange("p c x -> p (c x)"),
                       w12s[:, 1, :])

            def load3(u, cb, ic):
                t.dma("SPQ", w3st[:, ic % 4, :], fw3[u, cb, ic])
                t.copy("POOL", w3b[:, cb % 2, ic, :], w3st[:, ic % 4, :])

            load12(0, 0)
            for u in range(NU):
                for ic in range(NIC):
                    rp = ic % 2
                    if ic + 1 < NIC:
                        load12(u, ic + 1)
                    for hf in range(2):
                        for cc in range(NCC):
                            t.mm(h1p[hf][:, :], w1b[:, rp, cc, :],
                                 ln2t[:, cc, hf * 512:(hf + 1) * 512],
                                 start=(cc == 0), stop=(cc == NCC - 1),
                                 inc=(cc == NCC - 1))
                        for cc in range(NCC):
                            t.mm(h2p[hf][:, :], w2b[:, rp, cc, :],
                                 ln2t[:, cc, hf * 512:(hf + 1) * 512],
                                 start=(cc == 0), stop=(cc == NCC - 1),
                                 inc=(cc == NCC - 1))
                        t.act(s1b[:, hf, :], h1p[hf][:, :], AF.Silu)
                        t.tt("DVE", h_bf[:, ic, hf * 512:(hf + 1) * 512],
                             h2p[hf][:, :], s1b[:, hf, :], ALU.mult)
                for ic in range(NIC):
                    load3(u, 0, ic)
                for cb in range(4):
                    wp = cb % 2
                    if cb + 1 < 4:
                        for ic in range(NIC):
                            load3(u, cb + 1, ic)
                    elif u + 1 < NU:
                        load12(u + 1, 0)
                    for sa in range(SP):
                        p = eop[sa % 2]
                        for ic in range(NIC):
                            t.mm(p[:, :], h_bf[:, ic, sa * 128:(sa + 1) * 128],
                                 w3b[:, wp, ic, :], start=(ic == 0),
                                 stop=(ic == NIC - 1), inc=(ic == NIC - 1))
                        ysl = y[:, sa, cb * 512:(cb + 1) * 512]
                        if u == 0:
                            t.copy("DVE", ysl, p[:, :])
                        elif u == 1:
                            t.tt("DVE", ysl, p[:, :], ysl, ALU.add)
                        else:
                            t.stt("DVE", ysl, p[:, :], comb[:, sa, u - 2:u - 1],
                                  ysl, ALU.mult, ALU.add)

        t.barrier()

        with ExitStack() as p7:
            e7 = p7.enter_context
            x2r = e7(nc.sbuf_tensor("x2r", [128, 2, D], F32))
            outr = e7(nc.sbuf_tensor("outr", [128, 2, D], F32))
            t.dma("SPQ", x2r[:, 0, :], x2_d[0])
            t.dma("SPQ", x2r[:, 1, :], x2_d[1])
            for sa in range(SP):
                ysl = y[:, sa, :]
                ln_stats(ysl, sa % 2)
                o = outr[:, sa % 2, :]
                t.ts("DVE", o, ysl, mv[:, sa % 2, 0:1],
                     rstd[:, sa % 2:sa % 2 + 1], ALU.subtract, ALU.mult)
                t.tt("DVE", o, o, x2r[:, sa % 2, :], ALU.add)
                t.dma("ACTQ", out_d[sa], o)
                if sa + 2 < SP:
                    t.dma("SPQ", x2r[:, sa % 2, :], x2_d[sa + 2])
        y_scope.close()

        t.finish()
    return nc, dbg_aps



def _prep_shared(L0_attn_w_o, L0_attn_b_o, L0_ffn_gate_w,
                 L0_ffn_w1, L0_ffn_w2, L0_ffn_w3,
                 L0_ffn_sw1, L0_ffn_sw2, L0_ffn_sw3):
    wo_p = np.ascontiguousarray(
        L0_attn_w_o.reshape(NCC, 128, 4, 512).transpose(2, 1, 0, 3))
    bo_r = np.ascontiguousarray(L0_attn_b_o.reshape(1, D))
    gwt_p = np.ascontiguousarray(
        L0_ffn_gate_w.T.reshape(NCC, 128, 8).transpose(1, 0, 2))

    def prep_w12(w_sh, w_ex):
        sh = w_sh.T.reshape(NCC, 128, 2, NIC, 128).transpose(2, 3, 1, 0, 4)
        ex = np.stack([w_ex[ei].T.reshape(NCC, 128, NIC, 128).transpose(2, 1, 0, 3)
                       for ei in range(E)])
        return np.ascontiguousarray(
            np.concatenate([sh, ex], axis=0).reshape(NU, NIC, 128, D))

    fw1 = prep_w12(L0_ffn_sw1, L0_ffn_w1)
    fw2 = prep_w12(L0_ffn_sw2, L0_ffn_w2)

    sh3 = L0_ffn_sw3.T.reshape(2, NIC, 128, 4, 512).transpose(0, 3, 1, 2, 4)
    ex3 = np.stack([L0_ffn_w3[ei].T.reshape(NIC, 128, 4, 512).transpose(2, 0, 1, 3)
                    for ei in range(E)])
    fw3 = np.ascontiguousarray(np.concatenate([sh3, ex3], axis=0))

    masks = np.zeros((128, 4, 512), np.float32)
    for j in range(4):
        for ki in range(128):
            masks[ki, j, :128 * j + ki] = NEG
    masks = masks.astype(ml_dtypes.bfloat16)
    ident = np.eye(128, dtype=ml_dtypes.bfloat16)
    ones_b = np.ones((128, 128), ml_dtypes.bfloat16)
    ones_f = np.ones((128, 128), np.float32)
    return dict(wo_p=wo_p, bo_r=bo_r, gwt_p=gwt_p, fw1=fw1, fw2=fw2, fw3=fw3,
                masks=masks, ident=ident, ones_b=ones_b, ones_f=ones_f)


def _prep_core(c, x, w_q, b_q, w_k, b_k, w_v, b_v):
    own = np.concatenate([x[c * CH:(c + 1) * CH],
                          x[(15 - c) * CH:(16 - c) * CH]], axis=0)
    x_own = np.ascontiguousarray(own.reshape(SP, 128, D))
    cols = slice(256 * c, 256 * (c + 1))

    def wprep(w):
        return np.ascontiguousarray(
            w[:, cols].reshape(NCC, 128, 256).transpose(1, 0, 2))

    bq = b_q[cols].reshape(2, 128).T
    bk = b_k[cols].reshape(2, 128).T
    bqk = np.ascontiguousarray(
        np.stack([bq[:, 0], bq[:, 1], bk[:, 0], bk[:, 1]], axis=1))
    bv_m = np.ascontiguousarray(b_v[cols].reshape(1, 256))
    oidx = np.zeros((128, 32), np.int32)
    q1 = 15 - c
    piece = 0 if q1 < 12 else 1
    within = q1 - 8 - 4 * piece
    for cc in range(NCC):
        for p in range(128):
            oidx[p, cc * 2 + 0] = (cc * 128 + p) * 8 + c
            oidx[p, cc * 2 + 1] = piece * 8192 + (cc * 128 + p) * 4 + within
    return dict(x_own=x_own, wq_m=wprep(w_q), wk_m=wprep(w_k), wv_m=wprep(w_v),
                bqk_m=bqk, bv_m=bv_m, oidx=oidx)


_BUILD_CACHE = {}


def kernel(L0_attn_query, L0_attn_w_q, L0_attn_b_q, L0_attn_w_k, L0_attn_b_k,
           L0_attn_w_v, L0_attn_b_v, L0_attn_w_o, L0_attn_b_o, L0_ffn_gate_w,
           L0_ffn_w1, L0_ffn_w2, L0_ffn_w3, L0_ffn_sw1, L0_ffn_sw2, L0_ffn_sw3,
           _debug_outputs=(), _return_raw=False):
    x = np.asarray(L0_attn_query, np.float32).reshape(T, D)
    shared = _prep_shared(np.asarray(L0_attn_w_o, np.float32),
                          np.asarray(L0_attn_b_o, np.float32),
                          np.asarray(L0_ffn_gate_w, np.float32),
                          np.asarray(L0_ffn_w1, np.float32),
                          np.asarray(L0_ffn_w2, np.float32),
                          np.asarray(L0_ffn_w3, np.float32),
                          np.asarray(L0_ffn_sw1, np.float32),
                          np.asarray(L0_ffn_sw2, np.float32),
                          np.asarray(L0_ffn_sw3, np.float32))
    in_maps = []
    for c in range(NCORE):
        m = dict(shared)
        m.update(_prep_core(c, x,
                            np.asarray(L0_attn_w_q, np.float32),
                            np.asarray(L0_attn_b_q, np.float32),
                            np.asarray(L0_attn_w_k, np.float32),
                            np.asarray(L0_attn_b_k, np.float32),
                            np.asarray(L0_attn_w_v, np.float32),
                            np.asarray(L0_attn_b_v, np.float32)))
        in_maps.append(m)

    key = tuple(sorted(_debug_outputs))
    if key not in _BUILD_CACHE:
        _BUILD_CACHE[key] = build(_debug_outputs)
    nc, dbg_aps = _BUILD_CACHE[key]

    res = run_bass_kernel_spmd(nc, in_maps, core_ids=list(range(NCORE)))
    if _return_raw:
        return res
    full = np.empty((T, D), np.float32)
    for c in range(NCORE):
        o = res.results[c]["out"].reshape(SP * 128, D)
        full[c * CH:(c + 1) * CH] = o[:CH]
        full[(15 - c) * CH:(16 - c) * CH] = o[CH:]
    return full.reshape(1, T, D)
